# version 19 (final)
# baseline (speedup 1.0000x reference)
from contextlib import ExitStack

import numpy as np
import ml_dtypes

import concourse.bass as bass
import concourse.mybir as mybir
from concourse.bass_utils import run_bass_kernel_spmd

F32 = mybir.dt.float32
BF16 = mybir.dt.bfloat16
AF = mybir.ActivationFunctionType
ALU = mybir.AluOpType
AX = mybir.AxisListType

T = 8192
C = 1024
NCH = 8
BLK = 512
NBLK = T // BLK
NT = BLK // 128
NTILE = T // 128
ES = 256
NE = 16
HE = 256
ATT_SCALE = 1.0 / float(np.sqrt(np.float32(1024 * 2048)))
EPS = 1e-5
INV_C = 1.0 / C


class Sem:
    __slots__ = ("h", "n", "name")

    def __init__(self, nc, ctx, name):
        self.h = ctx.enter_context(nc.semaphore(name))
        self.n = 0
        self.name = name

    def inc(self, instr, k=1):
        instr.then_inc(self.h, k)
        self.n += k
        return instr


DATA_OPS = {
    "activation", "copy", "mul", "add", "square", "sqrt", "sign",
    "tensor_copy", "tensor_tensor", "tensor_add", "tensor_sub", "tensor_mul",
    "tensor_max", "tensor_scalar", "tensor_single_scalar", "tensor_scalar_add",
    "tensor_scalar_sub", "tensor_scalar_mul", "tensor_scalar_max",
    "tensor_scalar_min", "tensor_relu", "tensor_reduce", "reduce_max",
    "reduce_sum", "scalar_tensor_tensor", "memset", "memzero", "reciprocal",
    "bn_stats", "bn_aggr", "select", "iota",
}


class Chain:

    def __init__(self, eng, sem):
        self._e = eng
        self._s = sem

    def __getattr__(self, name):
        f = getattr(self._e, name)
        if name in DATA_OPS:
            s = self._s
            e = self._e

            def g(*a, **k):
                if s.n > 0:
                    e.wait_ge(s.h, s.n)
                return s.inc(f(*a, **k))

            return g
        return f


def build_nc(nblk=NBLK, debug=False, xnt_pat=True, lag=True, exact_ow=True):
    t_tok = nblk * BLK
    ntile = t_tok // 128
    nc = bass.Bass("TRN2", target_bir_lowering=False)

    x_in = nc.dram_tensor("x", [t_tok, C], F32, kind="ExternalInput").ap()
    wsrcs = [nc.dram_tensor(n, [C, C], F32, kind="ExternalInput").ap()
             for n in ["w1q", "w2q", "w1k", "w2k", "w1v", "w2v"]]
    b1q = nc.dram_tensor("b1q", [128, NCH], F32, kind="ExternalInput").ap()
    b2q = nc.dram_tensor("b2q", [128, NCH], F32, kind="ExternalInput").ap()
    b1k = nc.dram_tensor("b1k", [128, NCH], F32, kind="ExternalInput").ap()
    b2k = nc.dram_tensor("b2k", [128, NCH], F32, kind="ExternalInput").ap()
    b1v = nc.dram_tensor("b1v", [128, NCH], F32, kind="ExternalInput").ap()
    b2v = nc.dram_tensor("b2v", [1, C], F32, kind="ExternalInput").ap()
    lnw = nc.dram_tensor("lnw", [128, C], F32, kind="ExternalInput").ap()
    lnb = nc.dram_tensor("lnb", [128, C], F32, kind="ExternalInput").ap()
    g3 = nc.dram_tensor("g3", [128, C], F32, kind="ExternalInput").ap()
    b3 = nc.dram_tensor("b3", [128, C], F32, kind="ExternalInput").ap()
    ffnscale = nc.dram_tensor("ffnscale", [128, 1], F32, kind="ExternalInput").ap()
    mu_in = nc.dram_tensor("mu", [C, ES], F32, kind="ExternalInput").ap()
    w1e_in = nc.dram_tensor("w1e", [NE, C, HE], F32, kind="ExternalInput").ap()
    b1e_in = nc.dram_tensor("b1e", [1, NE * HE], F32, kind="ExternalInput").ap()
    w2e_in = nc.dram_tensor("w2e", [NE, HE, C], F32, kind="ExternalInput").ap()
    b2e_in = nc.dram_tensor("b2e", [1, NE * C], F32, kind="ExternalInput").ap()
    mask_in = nc.dram_tensor("mask", [128, NT, BLK], BF16, kind="ExternalInput").ap()
    ident_in = nc.dram_tensor("ident", [128, 128], BF16, kind="ExternalInput").ap()
    onesr_in = nc.dram_tensor("onesr", [1, 128], BF16, kind="ExternalInput").ap()
    onesc_in = nc.dram_tensor("onesc", [128, 1], BF16, kind="ExternalInput").ap()
    out_d = nc.dram_tensor("OUT", [t_tok, C], F32, kind="ExternalOutput").ap()

    dbg_kind = "ExternalOutput" if debug else "Internal"
    xb = nc.dram_tensor("xb", [t_tok, C], BF16, kind="Internal").ap()
    wb = [nc.dram_tensor(f"wb{i}", [C, C], BF16, kind="Internal").ap() for i in range(6)]
    xprime = nc.dram_tensor("xprime", [t_tok, C], BF16, kind=dbg_kind).ap()
    expb = nc.dram_tensor("expb", [t_tok, ES], BF16, kind=dbg_kind).ap()
    w1eb = nc.dram_tensor("w1eb", [NE * C, HE], BF16, kind="Internal").ap()
    w2eb = nc.dram_tensor("w2eb", [NE * HE, C], BF16, kind="Internal").ap()
    sodram = nc.dram_tensor("sodram", [ES, C], BF16, kind=dbg_kind).ap()
    dbg_S = nc.dram_tensor("dbg_S", [nblk * 128, NCH * C], BF16, kind=dbg_kind).ap() if debug else None
    dbg_q = nc.dram_tensor("dbg_q", [nblk * 128, NCH * BLK], BF16, kind=dbg_kind).ap() if debug else None
    dbg_at = nc.dram_tensor("dbg_at", [nblk * 128, NT * BLK], BF16, kind=dbg_kind).ap() if debug else None
    dbg_v = nc.dram_tensor("dbg_v", [nblk * 128, NT * C], BF16, kind=dbg_kind).ap() if debug else None
    dbg_x = nc.dram_tensor("dbg_x", [nblk * 128, NT * C], BF16, kind=dbg_kind).ap() if debug else None

    ctx = ExitStack()
    pSA = ctx.enter_context(nc.psum_tensor("pSA", [128, 512], F32))
    pSB = ctx.enter_context(nc.psum_tensor("pSB", [128, 512], F32))
    pAT = ctx.enter_context(nc.psum_tensor("pAT", [128, 512], F32))
    pTR = ctx.enter_context(nc.psum_tensor("pTR", [128, 1024], BF16))
    pOa = ctx.enter_context(nc.psum_tensor("pOa", [128, 1024], F32))
    pOb = ctx.enter_context(nc.psum_tensor("pOb", [128, 1024], F32))

    S32 = ctx.enter_context(nc.sbuf_tensor("S32", [128, NCH, C], F32))
    Sbf = ctx.enter_context(nc.sbuf_tensor("Sbf", [128, NCH, C], BF16))
    slot_in = ctx.enter_context(nc.sbuf_tensor("slot_in", [128, 2, C], F32))
    den_col = ctx.enter_context(nc.sbuf_tensor("den_col", [128, 2], F32))
    rcden = ctx.enter_context(nc.sbuf_tensor("rcden", [128, ntile], F32))
    mu_u = ctx.enter_context(nc.sbuf_tensor("mu_u", [128, NCH, ES], BF16))
    masks = ctx.enter_context(nc.sbuf_tensor("masksS", [128, NT, BLK], BF16))
    ident = ctx.enter_context(nc.sbuf_tensor("identS", [128, 128], BF16))
    onesr = ctx.enter_context(nc.sbuf_tensor("onesrS", [1, 128], BF16))
    onesc = ctx.enter_context(nc.sbuf_tensor("onescS", [128, 1], BF16))
    lnw_b = ctx.enter_context(nc.sbuf_tensor("lnw_b", [128, C], BF16))
    lnb_b = ctx.enter_context(nc.sbuf_tensor("lnb_b", [128, C], BF16))
    g3_b = ctx.enter_context(nc.sbuf_tensor("g3_b", [128, C], BF16))
    b3_b = ctx.enter_context(nc.sbuf_tensor("b3_b", [128, C], BF16))
    sc_bc = ctx.enter_context(nc.sbuf_tensor("sc_bc", [128, 1], F32))
    b1q_s = ctx.enter_context(nc.sbuf_tensor("b1q_s", [128, NCH], F32))
    b2q_s = ctx.enter_context(nc.sbuf_tensor("b2q_s", [128, NCH], F32))
    b1k_s = ctx.enter_context(nc.sbuf_tensor("b1k_s", [128, NCH], F32))
    b2k_s = ctx.enter_context(nc.sbuf_tensor("b2k_s", [128, NCH], F32))
    b1v_s = ctx.enter_context(nc.sbuf_tensor("b1v_s", [128, NCH], F32))
    b2v_r = ctx.enter_context(nc.sbuf_tensor("b2v_r", [1, C], BF16))
    st8 = ctx.enter_context(nc.sbuf_tensor("st8", [128, 32], F32))
    eps_c = ctx.enter_context(nc.sbuf_tensor("eps_c", [128, 1], F32))
    eps24_c = ctx.enter_context(nc.sbuf_tensor("eps24_c", [128, 1], F32))

    d_x = Sem(nc, ctx, "d_x")
    d_c = Sem(nc, ctx, "d_c")
    d_xA = Sem(nc, ctx, "d_xA")
    d_xB = Sem(nc, ctx, "d_xB")
    d_wA = Sem(nc, ctx, "d_wA")
    d_wB = Sem(nc, ctx, "d_wB")
    d_fm = Sem(nc, ctx, "d_fm")
    d_sp = Sem(nc, ctx, "d_sp")
    d_sc = Sem(nc, ctx, "d_sc")
    d_g = Sem(nc, ctx, "d_g")
    d_out = Sem(nc, ctx, "d_out")
    pe1 = Sem(nc, ctx, "pe1")
    peA = Sem(nc, ctx, "peA")
    peT = Sem(nc, ctx, "peT")
    peO = Sem(nc, ctx, "peO")
    peS = Sem(nc, ctx, "peS")
    peF = Sem(nc, ctx, "peF")
    A = Sem(nc, ctx, "A")
    V = Sem(nc, ctx, "V")
    P = Sem(nc, ctx, "P")

    class Waiter:
        def __init__(self):
            self.seen = {}

        def __call__(self, eng, ename, sem, val=None):
            v = sem.n if val is None else val
            if v is None or v <= 0:
                return
            key = (ename, sem.name)
            if self.seen.get(key, -1) >= v:
                return
            self.seen[key] = v
            eng.wait_ge(sem.h, v)

    W = Waiter()
    block = ctx.enter_context(nc.Block())

    SET_LNEXP = 6
    SET_GELU = 10

    def pin_act(raw, set_id):
        raw.add_instruction(mybir.InstLoadActFuncSet(
            name=nc.get_next_instruction_name(), act_func_set_id=set_id,
            ins=[], outs=[]))

    def scalar_section(fn):
        @block.scalar
        def _(sc):
            fn(Chain(sc, A), sc)

    def vector_section(fn):
        @block.vector
        def _(v):
            fn(Chain(v, V), v)

    def pool_section(fn):
        @block.gpsimd
        def _(g):
            fn(Chain(g, P), g)

    wctx = ExitStack()
    wbufA = wctx.enter_context(nc.sbuf_tensor("wbufA", [128, NCH, C], BF16))
    wbufB = wctx.enter_context(nc.sbuf_tensor("wbufB", [128, NCH, C], BF16))
    x_fm = wctx.enter_context(nc.sbuf_tensor("x_fm", [128, NCH, BLK], BF16))
    x_tm = wctx.enter_context(nc.sbuf_tensor("x_tm", [128, NT, C], BF16))
    hbuf = wctx.enter_context(nc.sbuf_tensor("hbuf", [128, NCH, BLK], BF16))
    q_fm = wctx.enter_context(nc.sbuf_tensor("q_fm", [128, NCH, BLK], BF16))
    k_fm = wctx.enter_context(nc.sbuf_tensor("k_fm", [128, NCH, BLK], BF16))
    k_tm = wctx.enter_context(nc.sbuf_tensor("k_tm", [128, NT, C], BF16))
    v_tm = wctx.enter_context(nc.sbuf_tensor("v_tm", [128, NT, C], BF16))
    at_s = wctx.enter_context(nc.sbuf_tensor("at_s", [128, NT, BLK], BF16))
    xp_t = wctx.enter_context(nc.sbuf_tensor("xp_t", [128, 2, C], BF16))
    xn_bf = wctx.enter_context(nc.sbuf_tensor("xn_bf", [128, NT, C], BF16))
    rl2s4 = wctx.enter_context(nc.sbuf_tensor("rl2s4", [128, NT], F32))
    xn_fm = wctx.enter_context(nc.sbuf_tensor("xn_fm", [128, NCH, 128], BF16))
    exp_t = wctx.enter_context(nc.sbuf_tensor("exp_t", [128, NT, ES], BF16))
    tA = wctx.enter_context(nc.sbuf_tensor("tA", [128, 2, C], F32))
    tB = wctx.enter_context(nc.sbuf_tensor("tB", [128, 2, C], F32))
    jb = wctx.enter_context(nc.sbuf_tensor("jb", [128, C], BF16))
    muf = tA[:, :, :].rearrange("p a b -> p (a b)").rearrange("p (kc es) -> p kc es", es=ES)
    musq = hbuf[:, :, 0:ES]
    mrow = tB[0:1, 0, 0:ES]
    mrow2 = jb[0:1, 0:ES]
    mbc = at_s[:, 0, 0:ES]

    S1, S2, MEAN, S2C, NVAR, RSTD, NMR, CDR, RL2 = range(9)

    pe1_after_stage = {}
    stage_evac_end = {}
    v_after_epi = {}
    a_stage_end = {}
    v_after_sadd = {}
    v_xnfm = {}
    v_after_scast = {}
    a_tile_release = {}
    v_epiA_end = {}
    a_xns_ready = {}
    v_ffn_end = {}
    pef_slotin = {}
    pet_xnt = {}

    def emit_stats():
        def fv(v, raw):
            W(raw, "v", A)
            v.tensor_scalar_mul(out=st8[:, MEAN:MEAN + 1], in0=st8[:, S1:S1 + 1], scalar1=INV_C)
            v.tensor_scalar_mul(out=st8[:, S2C:S2C + 1], in0=st8[:, S2:S2 + 1], scalar1=INV_C)
            v.scalar_tensor_tensor(out=st8[:, NVAR:NVAR + 1], in0=st8[:, MEAN:MEAN + 1],
                                   scalar=st8[:, MEAN:MEAN + 1], in1=st8[:, S2C:S2C + 1],
                                   op0=ALU.mult, op1=ALU.subtract)
        vector_section(fv)

        def fs(sc, raw):
            W(raw, "sc", V)
            sc.activation(out=st8[:, RSTD:RSTD + 1], in_=st8[:, NVAR:NVAR + 1],
                          func=AF.Ln, scale=-1.0, bias=eps_c[:, 0:1])
            sc.activation(out=st8[:, RSTD:RSTD + 1], in_=st8[:, RSTD:RSTD + 1],
                          func=AF.Exp, scale=-0.5)
        scalar_section(fs)

        def fv2(v, raw):
            W(raw, "v", A)
            v.tensor_scalar(out=st8[:, NMR:NMR + 1], in0=st8[:, MEAN:MEAN + 1],
                            scalar1=st8[:, RSTD:RSTD + 1], scalar2=-1.0,
                            op0=ALU.mult, op1=ALU.mult)
        vector_section(fv2)

    def f_pro_g(g, raw):
        d_g.inc(raw.dma_start(out=xb, in_=x_in), 16)
        for i in range(6):
            d_g.inc(raw.dma_start(out=wb[i], in_=wsrcs[i]), 16)
        d_g.inc(raw.dma_start(out=lnw_b[:, :], in_=lnw), 16)
        d_g.inc(raw.dma_start(out=lnb_b[:, :], in_=lnb), 16)
        d_g.inc(raw.dma_start(out=g3_b[:, :], in_=g3), 16)
        d_g.inc(raw.dma_start(out=b3_b[:, :], in_=b3), 16)
        d_g.inc(raw.dma_start(out=b2v_r[:, :], in_=b2v), 16)
        g.memset(eps_c[:, :], EPS)
        g.memset(eps24_c[:, :], 1e-24)
        g.memset(S32[:, :, :], 0.0)
        g.memset(Sbf[:, :, :], 0.0)
        g.memset(slot_in[:, :, :], 0.0)
        g.memset(den_col[:, :], 0.0)
        raw.wait_ge(d_g.h, d_g.n)
        d_g.inc(raw.dma_start(out=w1eb, in_=w1e_in.rearrange("e d h -> (e d) h")), 16)
        d_g.inc(raw.dma_start(out=w2eb, in_=w2e_in.rearrange("e h d -> (e h) d")), 16)
    pool_section(f_pro_g)
    d_g_main = d_g.n - 32
    p_memsets = P.n

    @block.sync
    def _(s):
        d_c.inc(s.dma_start(out=masks[:, :, :], in_=mask_in), 16)
        d_c.inc(s.dma_start(out=ident[:, :], in_=ident_in), 16)
        d_c.inc(s.dma_start(out=onesr[:, :], in_=onesr_in), 16)
        d_c.inc(s.dma_start(out=onesc[:, :], in_=onesc_in), 16)
        d_c.inc(s.dma_start(out=sc_bc[:, :], in_=ffnscale), 16)
        d_c.inc(s.dma_start(out=b1q_s[:, :], in_=b1q), 16)
        d_c.inc(s.dma_start(out=b2q_s[:, :], in_=b2q), 16)
        d_c.inc(s.dma_start(out=b1k_s[:, :], in_=b1k), 16)
        d_c.inc(s.dma_start(out=b2k_s[:, :], in_=b2k), 16)
        d_c.inc(s.dma_start(out=b1v_s[:, :], in_=b1v), 16)
        d_c.inc(s.dma_start(out=muf[:, :, :],
                            in_=mu_in.rearrange("(kc p) es -> p kc es", p=128)), 16)

    def f_b2q(v, raw):
        W(raw, "v", d_c)
        v.tensor_scalar_mul(out=b2q_s[:, :], in0=b2q_s[:, :], scalar1=ATT_SCALE)
    vector_section(f_b2q)
    v_b2q = V.n

    def f_musq(sc, raw):
        pin_act(raw, SET_LNEXP)
        W(raw, "sc", d_c)
        sc.activation(out=musq[:, :, :], in_=muf[:, :, :], func=AF.Square)
    scalar_section(f_musq)

    @block.tensor
    def _(t):
        W(t, "t", A)
        W(t, "t", d_c)
        for kc in range(NCH):
            mm = t.matmul(pOa[0:1, 0:ES], onesc[:, 0:1], musq[:, kc, :],
                          start=(kc == 0), stop=(kc == NCH - 1))
        peF.inc(mm)

    def f_mmax(v, raw):
        W(raw, "v", peF)
        v.tensor_scalar_max(out=mrow[:, :], in0=pOa[0:1, 0:ES], scalar1=1e-24)
    vector_section(f_mmax)

    def f_mrstd(sc, raw):
        W(raw, "sc", V)
        sc.activation(out=mrow[:, :], in_=mrow[:, :], func=AF.Ln)
        sc.activation(out=mrow2[:, :], in_=mrow[:, :], func=AF.Exp, scale=-0.5)
    scalar_section(f_mrstd)

    @block.tensor
    def _(t):
        W(t, "t", A)
        peF.inc(t.matmul(pOb[:, 0:ES], onesr[0:1, :], mrow2[0:1, :], start=True, stop=True))

    def f_mbc(v, raw):
        W(raw, "v", peF)
        v.tensor_copy(out=mbc[:, :], in_=pOb[:, 0:ES])
    vector_section(f_mbc)
    v_prologue = V.n

    def f_muu(g, raw):
        W(raw, "g", V)
        g.tensor_tensor(out=mu_u[:, :, :], in0=muf[:, :, :],
                        in1=mbc[:, :].unsqueeze(1).broadcast_to([128, NCH, ES]),
                        op=ALU.mult)
    pool_section(f_muu)
    p_muu = P.n
    pef_prologue = peF.n

    def wbuf_for(G):
        return wbufA if G % 2 == 0 else wbufB

    def wsem_for(G):
        return d_wA if G % 2 == 0 else d_wB

    @block.sync
    def _(s):
        W(s, "s", d_g, d_g_main)
        for G in range(2):
            wsem_for(G).inc(s.dma_start(out=wbuf_for(G)[:, :, :],
                                        in_=wb[G % 6].rearrange("(kc p) n -> p kc n", p=128)), 16)

    def emit_xtm(bb):
        r0 = bb * BLK

        @block.sync
        def _(s, bb=bb, r0=r0):
            if bb > 0:
                W(s, "s", V, v_epiA_end[(bb - 1) * NT + NT - 1])
            else:
                W(s, "s", d_g, d_g_main)
            d_x.inc(s.dma_start(out=x_tm[:, :, :],
                                in_=xb[r0:r0 + BLK, :].rearrange("(ti p) d -> p ti d", p=128)), 16)

    def emit_xfm(bb):
        r0 = bb * BLK

        @block.sync
        def _(s, bb=bb, r0=r0):
            if bb > 0:
                W(s, "s", pe1, pe1_after_stage[6 * (bb - 1) + 4])
            else:
                W(s, "s", d_g, d_g_main)
            for dc in range(NCH):
                d_fm.inc(s.dma_start_transpose(x_fm[:, dc, :],
                                               xb[r0:r0 + BLK, dc * 128:(dc + 1) * 128]), 16)

    evac_pos = {}

    def emit_stage(bb, s_idx, part=None):
        parts = (0, 1) if part is None else (part,)
        for pt in parts:
            G = 6 * bb + s_idx
            wbuf = wbuf_for(G)
            src = x_fm if s_idx in (0, 2, 4) else hbuf
            if s_idx == 0:
                dst, func, bias, scl = hbuf, AF.Gelu, b1q_s, 1.0
            elif s_idx == 1:
                dst, func, bias, scl = q_fm, AF.Identity, b2q_s, ATT_SCALE
            elif s_idx == 2:
                dst, func, bias, scl = hbuf, AF.Gelu, b1k_s, 1.0
            elif s_idx == 3:
                dst, func, bias, scl = k_fm, AF.Identity, b2k_s, 1.0
            elif s_idx == 4:
                dst, func, bias, scl = hbuf, AF.Gelu, b1v_s, 1.0
            else:
                dst, func, bias, scl = v_tm, AF.Identity, None, 1.0
            lo, hi = 4 * pt, 4 * pt + 4
            for m in range(lo, hi):
                evac_pos[(G, m)] = A.n + (m - lo) + 1

            @block.tensor
            def _(t, G=G, s_idx=s_idx, wbuf=wbuf, src=src, bb=bb, lo=lo, hi=hi):
                W(t, "t", wsem_for(G), 16 * (G // 2 + 1))
                if lo == 0:
                    W(t, "t", d_fm)
                    W(t, "t", d_c)
                    W(t, "t", V, v_after_sadd[bb - 1] if bb > 0 else None)
                for m in range(lo, hi):
                    if m < 2:
                        if s_idx in (1, 3, 5):
                            W(t, "t", A, evac_pos[(G - 1, 7)])
                        elif s_idx > 0:
                            W(t, "t", A, evac_pos[(G - 1, 6 + m)])
                    else:
                        W(t, "t", A, evac_pos[(G, m - 2)])
                    p = pSA if m % 2 == 0 else pSB
                    if s_idx < 5:
                        for kc in range(NCH):
                            mm = t.matmul(p[:, :], wbuf[:, kc, m * 128:(m + 1) * 128],
                                          src[:, kc, :], start=(kc == 0), stop=(kc == NCH - 1))
                    else:
                        ti, dvh = divmod(m, 2)
                        for kc in range(NCH):
                            t.matmul(p[:, :], src[:, kc, ti * 128:(ti + 1) * 128],
                                     wbuf[:, kc, dvh * 512:(dvh + 1) * 512],
                                     start=(kc == 0), stop=False)
                        mm = t.matmul(p[:, :], onesr[0:1, :],
                                      b2v_r[0:1, dvh * 512:(dvh + 1) * 512],
                                      start=False, stop=True)
                    pe1.inc(mm)
                if hi == 8:
                    pe1_after_stage[G] = pe1.n

            if hi == 8 and G + 2 < 6 * nblk:
                @block.sync
                def _(s, G=G):
                    W(s, "s", pe1, pe1_after_stage[G])
                    wsem_for(G).inc(s.dma_start(out=wbuf_for(G + 2)[:, :, :],
                                                in_=wb[(G + 2) % 6].rearrange("(kc p) n -> p kc n", p=128)), 16)

            def f_evac(sc, raw, G=G, s_idx=s_idx, dst=dst, func=func, bias=bias,
                       scl=scl, lo=lo, hi=hi):
                W(raw, "sc", V, v_b2q)
                if G == 0:
                    W(raw, "sc", peF, pef_prologue)
                base = pe1.n - (hi - lo)
                for m in range(lo, hi):
                    W(raw, "sc", pe1, base + (m - lo) + 1)
                    p = pSA if m % 2 == 0 else pSB
                    if s_idx < 5:
                        kw = {}
                        if bias is not None:
                            kw["bias"] = bias[:, m:m + 1]
                        if scl != 1.0:
                            kw["scale"] = scl
                        sc.activation(out=dst[:, m, :], in_=p[:, :], func=func, **kw)
                    else:
                        ti, dvh = divmod(m, 2)
                        sc.activation(out=dst[:, ti, dvh * 512:(dvh + 1) * 512],
                                      in_=p[:, :], func=AF.Identity)
            scalar_section(f_evac)
            assert A.n == evac_pos[(G, hi - 1)], (G, lo, A.n, evac_pos[(G, hi - 1)])
            if hi == 8:
                stage_evac_end[G] = A.n
                if s_idx == 5:
                    a_stage_end[bb] = A.n

    act_state = ["L"]

    def emit_pin_sec(set_id):
        @block.scalar
        def _(sc):
            pin_act(sc, set_id)

    def EMIT(cls, fn):
        if cls == "G" and act_state[0] != "G":
            emit_pin_sec(SET_GELU)
            act_state[0] = "G"
        elif cls == "L" and act_state[0] != "L":
            emit_pin_sec(SET_LNEXP)
            act_state[0] = "L"
        fn()

    def stage_cls(s_idx):
        return "G" if s_idx in (0, 2, 4) else "N"

    emit_xtm(0)
    emit_xfm(0)
    for _s in range(6):
        EMIT(stage_cls(_s), lambda s=_s: emit_stage(0, s))

    v_atev = {}
    v_kevac = {}
    peo_tile = {}

    def emit_kT(bb):
        @block.tensor
        def _(t, bb=bb):
            W(t, "t", A)
            W(t, "t", V)
            v_base = V.n
            for jt in range(NT):
                if jt >= 1:
                    W(t, "t", V, v_base + jt)
                for dc in range(NCH):
                    mm = t.transpose(pTR[:, dc * 128:(dc + 1) * 128],
                                     k_fm[:, dc, jt * 128:(jt + 1) * 128], ident[:, :])
                peT.inc(mm)

        def f_kevac(v, raw, bb=bb):
            base = peT.n - NT
            for jt in range(NT):
                W(raw, "v", peT, base + jt + 1)
                v.tensor_copy(out=k_tm[:, jt, :], in_=pTR[:, :])
        vector_section(f_kevac)
        v_kevac[bb] = V.n

    def emit_AT(bb):
        @block.tensor
        def _(t, bb=bb):
            W(t, "t", V)
            v_base = V.n
            for jt in range(NT):
                if jt >= 1:
                    W(t, "t", V, v_base + jt)
                for kc in range(NCH):
                    mm = t.matmul(pAT[:, :], k_fm[:, kc, jt * 128:(jt + 1) * 128],
                                  q_fm[:, kc, :], start=(kc == 0), stop=(kc == NCH - 1))
                peA.inc(mm)

        def f_atevac(v, raw, bb=bb):
            base = peA.n - NT
            W(raw, "v", d_c)
            W(raw, "v", P, p_muu)
            for jt in range(NT):
                W(raw, "v", peA, base + jt + 1)
                v.tensor_tensor(out=at_s[:, jt, :], in0=pAT[:, :],
                                in1=masks[:, jt, :], op=ALU.mult)
        vector_section(f_atevac)
        v_atev[bb] = V.n

    def supd_segs(bb):
        segs = []
        supd_banks = [pSA[:, :], pSB[:, :]]
        v_supd = {}
        st_ctr = {}

        def one(g_i):
            dk, dvh = divmod(g_i, 2)
            if g_i >= 2:
                ga = g_i - 2

                def f_sadd(v, raw, ga=ga, bb=bb):
                    if ga == 0:
                        W(raw, "v", P, p_memsets)
                        W(raw, "v", d_sc, v_after_scast.get(bb - 1))
                    W(raw, "v", peS, st_ctr[ga])
                    da, dva = divmod(ga, 2)
                    v.tensor_tensor(out=S32[:, da, dva * 512:(dva + 1) * 512],
                                    in0=supd_banks[ga % 2],
                                    in1=S32[:, da, dva * 512:(dva + 1) * 512], op=ALU.add)
                vector_section(f_sadd)
                v_supd[ga] = V.n
            if g_i < NCH * 2:
                @block.tensor
                def _(t, g_i=g_i, dk=dk, dvh=dvh, bb=bb):
                    if g_i == 0:
                        W(t, "t", A, a_stage_end[bb])
                        W(t, "t", V, v_kevac[bb])
                        W(t, "t", P, p_memsets)
                    if g_i >= 2:
                        W(t, "t", V, v_supd[g_i - 2])
                    p = supd_banks[g_i % 2]
                    for jt in range(NT):
                        mm = t.matmul(p, k_tm[:, jt, dk * 128:(dk + 1) * 128],
                                      v_tm[:, jt, dvh * 512:(dvh + 1) * 512],
                                      start=(jt == 0), stop=(jt == NT - 1))
                    peS.inc(mm)
                st_ctr[g_i] = peS.n
            if g_i == NCH * 2 + 1:
                v_after_sadd[bb] = V.n
        for g_i in range(NCH * 2 + 2):
            segs.append(lambda g=g_i: one(g))
        return segs

    emit_kT(0)
    emit_AT(0)
    for _seg in supd_segs(0):
        _seg()

    for b in range(nblk):
        def st(par, col):
            return st8[:, col * 2 + par: col * 2 + par + 1]

        def stp(col):
            return st8[:, col * 2: col * 2 + 2]

        def emit_stats2(par):
            def fv(v, raw, par=par):
                W(raw, "v", A)
                v.tensor_scalar_mul(out=st(par, MEAN), in0=st(par, S1), scalar1=INV_C)
                v.tensor_scalar_mul(out=st(par, S2C), in0=st(par, S2), scalar1=INV_C)
                v.scalar_tensor_tensor(out=st(par, NVAR), in0=st(par, MEAN),
                                       scalar=st(par, MEAN), in1=st(par, S2C),
                                       op0=ALU.mult, op1=ALU.subtract)
            vector_section(fv)

            def fs(sc, raw, par=par):
                W(raw, "sc", V)
                sc.activation(out=st(par, RSTD), in_=st(par, NVAR),
                              func=AF.Ln, scale=-1.0, bias=eps_c[:, 0:1])
                sc.activation(out=st(par, RSTD), in_=st(par, RSTD),
                              func=AF.Exp, scale=-0.5)
            scalar_section(fs)

            def fv2(v, raw, par=par):
                W(raw, "v", A)
                v.tensor_scalar(out=st(par, NMR), in0=st(par, MEAN),
                                scalar1=st(par, RSTD), scalar2=-1.0,
                                op0=ALU.mult, op1=ALU.mult)
            vector_section(fv2)

        def emit_O(ti, b=b, v_at_evac=None):
            tile64 = b * NT + ti
            par = ti % 2
            pO = pOa if par == 0 else pOb

            @block.tensor
            def _(t, b=b, ti=ti, pO=pO, tile64=tile64, v_at_evac=v_at_evac):
                W(t, "t", V, v_at_evac if exact_ow else None)
                W(t, "t", V, v_ffn_end.get(b * NT - 1))
                W(t, "t", d_sc, v_after_scast.get(b - 1))
                W(t, "t", P, p_memsets)
                W(t, "t", P, p_muu)
                if tile64 >= 2:
                    W(t, "t", A, a_tile_release[tile64 - 2])
                for dvh in range(2):
                    sl = slice(dvh * 512, (dvh + 1) * 512)
                    for jt in range(NT):
                        t.matmul(pO[:, sl], at_s[:, jt, ti * 128:(ti + 1) * 128],
                                 v_tm[:, jt, sl], start=(jt == 0), stop=False)
                    for kc in range(NCH):
                        mm = t.matmul(pO[:, sl], q_fm[:, kc, ti * 128:(ti + 1) * 128],
                                      Sbf[:, kc, sl], start=False, stop=(kc == NCH - 1))
                peO.inc(mm)
            peo_tile[tile64] = peO.n

        def epiA_pair_segs(t0, b=b):
            segs = []
            ta, tb_ = t0, t0 + 1
            t64a, t64b = b * NT + ta, b * NT + tb_

            def f_e1a(sc, raw):
                W(raw, "sc", peO, peo_tile[t64a])
                W(raw, "sc", P, p_muu)
                W(raw, "sc", peF, pef_prologue)
                W(raw, "sc", V, v_epiA_end.get(t64a - 2))
                sc.activation(out=tA[:, 0, :], in_=pOa[:, :], func=AF.Copy,
                              accum_out=st(0, S1))
                sc.activation(out=jb[:, :], in_=pOa[:, :], func=AF.Square,
                              accum_out=st(0, S2))
            def s_e1a():
                scalar_section(f_e1a)
                a_tile_release[t64a] = A.n
            segs.append(s_e1a)

            def f_e1b(sc, raw):
                W(raw, "sc", peO, peo_tile[t64b])
                W(raw, "sc", V, v_epiA_end.get(t64b - 2))
                sc.activation(out=tA[:, 1, :], in_=pOb[:, :], func=AF.Copy,
                              accum_out=st(1, S1))
                sc.activation(out=jb[:, :], in_=pOb[:, :], func=AF.Square,
                              accum_out=st(1, S2))
                sc.activation(out=stp(MEAN), in_=stp(S1), func=AF.Identity,
                              scale=-INV_C)
                sc.activation(out=stp(S2C), in_=stp(S2), func=AF.Identity,
                              scale=INV_C)
            def s_e1b():
                scalar_section(f_e1b)
                a_tile_release[t64b] = A.n
            segs.append(s_e1b)

            def f_nv1(v, raw):
                W(raw, "v", A)
                v.tensor_tensor(out=stp(10), in0=stp(MEAN), in1=stp(MEAN), op=ALU.mult)
                v.tensor_tensor(out=stp(NVAR), in0=stp(10), in1=stp(S2C), op=ALU.subtract)
            segs.append(lambda: vector_section(f_nv1))

            def f_a1(sc, raw):
                W(raw, "sc", V)
                sc.activation(out=stp(RSTD), in_=stp(NVAR),
                              func=AF.Ln, scale=-1.0, bias=eps_c[:, 0:1])
                sc.activation(out=stp(RSTD), in_=stp(RSTD), func=AF.Exp, scale=-0.5)
                sc.activation(out=st(0, NMR), in_=st(0, MEAN), func=AF.Identity,
                              scale=st(0, RSTD))
                sc.activation(out=st(1, NMR), in_=st(1, MEAN), func=AF.Identity,
                              scale=st(1, RSTD))
                sc.activation(out=tB[:, 0, :], in_=tA[:, 0, :], func=AF.Identity,
                              scale=st(0, RSTD), bias=st(0, NMR))
                sc.activation(out=tB[:, 1, :], in_=tA[:, 1, :], func=AF.Identity,
                              scale=st(1, RSTD), bias=st(1, NMR))
            segs.append(lambda: scalar_section(f_a1))

            lnw_bc = lnw_b[:, :].unsqueeze(1).broadcast_to([128, 2, C])
            lnb_bc = lnb_b[:, :].unsqueeze(1).broadcast_to([128, 2, C])
            g3_bc = g3_b[:, :].unsqueeze(1).broadcast_to([128, 2, C])
            b3_bc = b3_b[:, :].unsqueeze(1).broadcast_to([128, 2, C])

            def f_e34(v, raw):
                W(raw, "v", A)
                W(raw, "v", peF, pef_prologue)
                W(raw, "v", d_x)
                v.tensor_tensor(out=tA[:, :, :], in0=tB[:, :, :], in1=lnw_bc, op=ALU.mult)
                v.tensor_tensor(out=tB[:, :, :], in0=tA[:, :, :], in1=lnb_bc, op=ALU.add)
                v.tensor_tensor(out=tA[:, :, :], in0=tB[:, :, :],
                                in1=x_tm[:, t0:t0 + 2, :], op=ALU.add)
                v.reduce_sum(out=st(0, S1), in_=tA[:, 0, :], axis=AX.X)
                v.reduce_sum(out=st(1, S1), in_=tA[:, 1, :], axis=AX.X)
            segs.append(lambda: vector_section(f_e34))

            def f_e5(sc, raw):
                W(raw, "sc", V)
                sc.activation(out=jb[:, :], in_=tA[:, 0, :], func=AF.Square,
                              accum_out=st(0, S2))
                sc.activation(out=jb[:, :], in_=tA[:, 1, :], func=AF.Square,
                              accum_out=st(1, S2))
                sc.activation(out=stp(MEAN), in_=stp(S1), func=AF.Identity,
                              scale=-INV_C)
                sc.activation(out=stp(S2C), in_=stp(S2), func=AF.Identity,
                              scale=INV_C)
            segs.append(lambda: scalar_section(f_e5))

            def f_nv2(v, raw):
                W(raw, "v", A)
                v.tensor_tensor(out=stp(10), in0=stp(MEAN), in1=stp(MEAN), op=ALU.mult)
                v.tensor_tensor(out=stp(NVAR), in0=stp(10), in1=stp(S2C), op=ALU.subtract)
            segs.append(lambda: vector_section(f_nv2))

            def f_a2(sc, raw):
                W(raw, "sc", V)
                W(raw, "sc", d_sp)
                sc.activation(out=stp(RSTD), in_=stp(NVAR),
                              func=AF.Ln, scale=-1.0, bias=eps_c[:, 0:1])
                sc.activation(out=stp(RSTD), in_=stp(RSTD), func=AF.Exp, scale=-0.5)
                sc.activation(out=st(0, NMR), in_=st(0, MEAN), func=AF.Identity,
                              scale=st(0, RSTD))
                sc.activation(out=st(1, NMR), in_=st(1, MEAN), func=AF.Identity,
                              scale=st(1, RSTD))
                sc.activation(out=xp_t[:, 0, :], in_=tA[:, 0, :], func=AF.Identity,
                              scale=st(0, RSTD), bias=st(0, NMR),
                              accum_out=st(0, S1))
                sc.activation(out=jb[:, :], in_=xp_t[:, 0, :], func=AF.Square,
                              accum_out=st(0, S2))
                sc.activation(out=xp_t[:, 1, :], in_=tA[:, 1, :], func=AF.Identity,
                              scale=st(1, RSTD), bias=st(1, NMR),
                              accum_out=st(1, S1))
                sc.activation(out=jb[:, :], in_=xp_t[:, 1, :], func=AF.Square,
                              accum_out=st(1, S2))
                sc.activation(out=stp(MEAN), in_=stp(S1), func=AF.Identity,
                              scale=-INV_C)
                sc.activation(out=stp(S2C), in_=stp(S2), func=AF.Identity,
                              scale=INV_C)
            segs.append(lambda: scalar_section(f_a2))

            def s_spill(ti):
                @block.gpsimd
                def _(g, ti=ti):
                    W(g, "g", A)
                    d_sp.inc(g.dma_start(out=xprime[b * BLK + ti * 128: b * BLK + (ti + 1) * 128, :],
                                         in_=xp_t[:, ti % 2, :]), 16)
            segs.append(lambda: s_spill(ta))
            segs.append(lambda: s_spill(tb_))

            def f_nv3(v, raw):
                W(raw, "v", A)
                v.tensor_tensor(out=stp(10), in0=stp(MEAN), in1=stp(MEAN), op=ALU.mult)
                v.tensor_tensor(out=stp(NVAR), in0=stp(10), in1=stp(S2C), op=ALU.subtract)
            segs.append(lambda: vector_section(f_nv3))

            def f_a3(sc, raw):
                W(raw, "sc", V)
                sc.activation(out=stp(RSTD), in_=stp(NVAR),
                              func=AF.Ln, scale=-1.0, bias=eps_c[:, 0:1])
                sc.activation(out=stp(RSTD), in_=stp(RSTD), func=AF.Exp, scale=-0.5)
                sc.activation(out=st(0, NMR), in_=st(0, MEAN), func=AF.Identity,
                              scale=st(0, RSTD))
                sc.activation(out=st(1, NMR), in_=st(1, MEAN), func=AF.Identity,
                              scale=st(1, RSTD))
                sc.activation(out=tA[:, 0, :], in_=xp_t[:, 0, :], func=AF.Identity,
                              scale=st(0, RSTD), bias=st(0, NMR))
                sc.activation(out=tA[:, 1, :], in_=xp_t[:, 1, :], func=AF.Identity,
                              scale=st(1, RSTD), bias=st(1, NMR))
            segs.append(lambda: scalar_section(f_a3))

            def f_e9(v, raw):
                W(raw, "v", A)
                W(raw, "v", peF, pef_slotin.get(t64a - 4))
                W(raw, "v", peF, pef_slotin.get(t64b - 4))
                W(raw, "v", peT, pet_xnt.get(t64a - 4))
                W(raw, "v", peT, pet_xnt.get(t64b - 4))
                v.tensor_tensor(out=tB[:, :, :], in0=tA[:, :, :], in1=g3_bc, op=ALU.mult)
                v.tensor_tensor(out=xn_bf[:, t0:t0 + 2, :], in0=tB[:, :, :],
                                in1=b3_bc, op=ALU.add)
            segs.append(lambda: vector_section(f_e9))

            def f_a4(sc, raw):
                W(raw, "sc", V)
                sc.activation(out=jb[:, :], in_=xn_bf[:, ta, :], func=AF.Square,
                              accum_out=st(0, S2))
                sc.activation(out=jb[:, :], in_=xn_bf[:, tb_, :], func=AF.Square,
                              accum_out=st(1, S2))
                sc.activation(out=stp(RL2), in_=stp(S2), func=AF.Ln,
                              bias=eps24_c[:, 0:1])
                sc.activation(out=stp(RL2), in_=stp(RL2), func=AF.Exp, scale=-0.5)
                sc.activation(out=rl2s4[:, t0:t0 + 2], in_=stp(RL2), func=AF.Identity,
                              scale=sc_bc[:, 0:1])
            def s_a4():
                scalar_section(f_a4)
                a_xns_ready[t64a] = A.n
                a_xns_ready[t64b] = A.n
                v_epiA_end[t64a] = V.n
                v_epiA_end[t64b] = V.n
            segs.append(s_a4)
            return segs

        def emit_ffn(ti, b=b):
            tile64 = b * NT + ti
            par = ti % 2
            cdr_c = CDR if ti < 2 else 9
            pX = pTR[:, :].bitcast(F32)
            pDen = pTR[:, 512:516].bitcast(F32)
            pATb = pAT[:, :].bitcast(BF16)

            @block.tensor
            def _(t, par=par, tile64=tile64, ti=ti, pATb=pATb):
                W(t, "t", A, a_xns_ready[tile64])
                if ti == 0:
                    W(t, "t", V, v_atev[b])
                    W(t, "t", V, v_ffn_end.get(tile64 - 1))
                elif ti == 2:
                    if (b + 1) in v_atev:
                        W(t, "t", V, v_atev[b + 1])
                    else:
                        W(t, "t", V, v_xnfm[tile64 - 1])
                else:
                    W(t, "t", V, v_xnfm[tile64 - 1])
                for dc in range(NCH):
                    mm = t.transpose(pATb[:, dc * 128:(dc + 1) * 128],
                                     xn_bf[:, ti, dc * 128:(dc + 1) * 128], ident[:, :])
                peT.inc(mm)
            pet_xnt[tile64] = peT.n

            def f_xnfm(v, raw, pATb=pATb):
                W(raw, "v", peT)
                v.tensor_copy(out=xn_fm[:, :, :].rearrange("p a b -> p (a b)"),
                              in_=pATb[:, :])
            vector_section(f_xnfm)
            v_xnfm[tile64] = V.n

            @block.tensor
            def _(t, pX=pX, tile64=tile64, ti=ti):
                W(t, "t", V, v_xnfm[tile64])
                if ti in (1, 2, 3):
                    W(t, "t", V, v_ffn_end.get(tile64 - 1))
                W(t, "t", P, p_muu)
                for dc in range(NCH):
                    mm = t.matmul(pX[:, 0:ES], xn_fm[:, dc, :], mu_u[:, dc, :],
                                  start=(dc == 0), stop=(dc == NCH - 1))
                peF.inc(mm)

            def f_exp(sc, raw, par=par, tile64=tile64, pX=pX, ti=ti, cdr_c=cdr_c):
                W(raw, "sc", peF)
                W(raw, "sc", d_sp)
                W(raw, "sc", V, v_ffn_end.get(tile64 - 4))
                sc.activation(out=exp_t[:, ti, :], in_=pX[:, 0:ES], func=AF.Exp,
                              scale=rl2s4[:, ti:ti + 1],
                              accum_out=st(par, cdr_c))
                sc.activation(out=st(par, 14), in_=st(par, cdr_c), func=AF.Identity)
            scalar_section(f_exp)

            def f_rcd(v, raw, tile64=tile64, par=par, cdr_c=cdr_c):
                W(raw, "v", A)
                v.reciprocal(out=rcden[:, tile64:tile64 + 1], in_=st(par, cdr_c))
            vector_section(f_rcd)

            @block.gpsimd
            def _(g, b=b, ti=ti, par=par):
                W(g, "g", A)
                d_sp.inc(g.dma_start(out=expb[b * BLK + ti * 128: b * BLK + (ti + 1) * 128, :],
                                     in_=exp_t[:, ti, :]), 16)

            @block.tensor
            def _(t, par=par, ti=ti, pDen=pDen):
                W(t, "t", A)
                W(t, "t", V)
                for ec in range(2):
                    mm = t.matmul(pDen[:, ec:ec + 1], exp_t[:, ti, ec * 128:(ec + 1) * 128],
                                  onesc[:, 0:1], start=True, stop=True)
                peF.inc(mm)

            def f_den(v, raw, pDen=pDen):
                W(raw, "v", peF)
                W(raw, "v", P, p_memsets)
                v.tensor_tensor(out=den_col[:, :], in0=pDen[:, 0:2],
                                in1=den_col[:, :], op=ALU.add)
            vector_section(f_den)

            for ec in range(2):
                for nh in range(2):
                    @block.tensor
                    def _(t, ec=ec, nh=nh, par=par, ti=ti, pX=pX):
                        W(t, "t", V)
                        W(t, "t", A)
                        peF.inc(t.matmul(pX[:, :], exp_t[:, ti, ec * 128:(ec + 1) * 128],
                                         xn_bf[:, ti, nh * 512:(nh + 1) * 512],
                                         start=True, stop=True))

                    def f_sl(v, raw, ec=ec, nh=nh, pX=pX):
                        W(raw, "v", peF)
                        v.tensor_tensor(
                            out=slot_in[:, ec, nh * 512:(nh + 1) * 512], in0=pX[:, :],
                            in1=slot_in[:, ec, nh * 512:(nh + 1) * 512], op=ALU.add)
                    vector_section(f_sl)
            pef_slotin[tile64] = peF.n
            v_ffn_end[tile64] = V.n

        CCLS = ["N", "N", "N", "L", "N", "N", "N", "L", "N", "N", "N", "L", "N", "L"]

        def crun(c, lo, hi):
            for k in range(lo, hi):
                EMIT(CCLS[k], c[k])

        def scast_dma(b=b):
            @block.gpsimd
            def _(g, b=b):
                W(g, "g", peO, peo_tile[b * NT + 3])
                d_sc.inc(g.dma_start(out=Sbf[:, :, :], in_=S32[:, :, :]), 16)
            v_after_scast[b] = d_sc.n

        if b == 0:
            emit_O(0, v_at_evac=v_atev[b])
            emit_O(1, v_at_evac=v_atev[b])
        c01 = epiA_pair_segs(0)
        crun(c01, 0, 2)
        emit_O(2, v_at_evac=v_atev[b])
        emit_O(3, v_at_evac=v_atev[b])
        scast_dma(b)


        c23 = epiA_pair_segs(2)
        nxt = b + 1 < nblk
        if nxt:
            emit_xfm(b + 1)
            crun(c01, 2, 3)
            EMIT("G", lambda: emit_stage(b + 1, 0))
            crun(c01, 3, 5)
            EMIT("N", lambda: emit_stage(b + 1, 1, 0))
            crun(c01, 5, 7)
            EMIT("N", lambda: emit_stage(b + 1, 1, 1))
            crun(c01, 7, 9)
            EMIT("G", lambda: emit_stage(b + 1, 2, 0))
            crun(c01, 9, 11)
            EMIT("G", lambda: emit_stage(b + 1, 2, 1))
            crun(c01, 11, 12)
            EMIT("N", lambda: emit_stage(b + 1, 3, 0))
            crun(c01, 12, 13)
            EMIT("N", lambda: emit_stage(b + 1, 3, 1))
            crun(c01, 13, 14)
            EMIT("L", lambda: emit_ffn(0))
            EMIT("L", lambda: emit_ffn(1))
            crun(c23, 0, 3)
            EMIT("G", lambda: emit_stage(b + 1, 4, 0))
            crun(c23, 3, 5)
            EMIT("G", lambda: emit_stage(b + 1, 4, 1))
            crun(c23, 5, 7)
            EMIT("N", lambda: emit_stage(b + 1, 5, 0))
            crun(c23, 7, 9)
            EMIT("N", lambda: emit_stage(b + 1, 5, 1))
            crun(c23, 9, 11)
            emit_kT(b + 1)
            crun(c23, 11, 12)
            emit_AT(b + 1)
            crun(c23, 12, 13)
            emit_O(0, b=b + 1, v_at_evac=v_atev[b + 1])
            crun(c23, 13, 14)
            emit_O(1, b=b + 1, v_at_evac=v_atev[b + 1])
            for s in supd_segs(b + 1):
                s()
            emit_xtm(b + 1)
        else:
            crun(c01, 2, 14)
            EMIT("L", lambda: emit_ffn(0))
            EMIT("L", lambda: emit_ffn(1))
            crun(c23, 0, 14)
        EMIT("L", lambda: emit_ffn(2))
        EMIT("L", lambda: emit_ffn(3))
        v_after_epi[b] = V.n

        if debug:
            @block.gpsimd
            def _(g, b=b):
                W(g, "g", V, v_after_scast[b])
                d_sp.inc(g.dma_start(out=dbg_S[b * 128:(b + 1) * 128, :],
                                     in_=Sbf[:, :, :].rearrange("p a c -> p (a c)")), 16)
                d_sp.inc(g.dma_start(out=dbg_q[b * 128:(b + 1) * 128, :],
                                     in_=q_fm[:, :, :].rearrange("p a c -> p (a c)")), 16)
                d_sp.inc(g.dma_start(out=dbg_at[b * 128:(b + 1) * 128, :],
                                     in_=at_s[:, :, :].rearrange("p a c -> p (a c)")), 16)
                d_sp.inc(g.dma_start(out=dbg_v[b * 128:(b + 1) * 128, :],
                                     in_=v_tm[:, :, :].rearrange("p a c -> p (a c)")), 16)
                d_sp.inc(g.dma_start(out=dbg_x[b * 128:(b + 1) * 128, :],
                                     in_=x_tm[:, :, :].rearrange("p a c -> p (a c)")), 16)

    wctx.close()
    bctx = ExitStack()
    w1A = bctx.enter_context(nc.sbuf_tensor("w1A", [128, NCH, HE], BF16))
    w1B = bctx.enter_context(nc.sbuf_tensor("w1B", [128, NCH, HE], BF16))
    w2A = bctx.enter_context(nc.sbuf_tensor("w2A", [128, 2, C], BF16))
    w2B = bctx.enter_context(nc.sbuf_tensor("w2B", [128, 2, C], BF16))
    b1e_r = bctx.enter_context(nc.sbuf_tensor("b1e_r", [1, NE * HE], BF16))
    b2e_r = bctx.enter_context(nc.sbuf_tensor("b2e_r", [1, NE * C], BF16))
    si_n = bctx.enter_context(nc.sbuf_tensor("si_n", [128, 2, C], BF16))
    si_fm = bctx.enter_context(nc.sbuf_tensor("si_fm", [128, NCH, ES], BF16))
    heb = bctx.enter_context(nc.sbuf_tensor("heb", [16, HE], BF16))
    heT = bctx.enter_context(nc.sbuf_tensor("heT", [128, 2, 16], BF16))
    so_st = bctx.enter_context(nc.sbuf_tensor("so_st", [16, C], BF16))
    so_sb = bctx.enter_context(nc.sbuf_tensor("so_sb", [128, 2, C], BF16))
    expT = bctx.enter_context(nc.sbuf_tensor("expT", [128, 2, 128], BF16))
    xp_in = bctx.enter_context(nc.sbuf_tensor("xp_in", [128, 2, C], BF16))
    exp_in = bctx.enter_context(nc.sbuf_tensor("exp_in", [128, 2, ES], BF16))
    out_sb = bctx.enter_context(nc.sbuf_tensor("out_sb", [128, 2, C], F32))
    tC = bctx.enter_context(nc.sbuf_tensor("tC", [128, 2, C], F32))
    jC = bctx.enter_context(nc.sbuf_tensor("jC", [128, C], BF16))
    rden = bctx.enter_context(nc.sbuf_tensor("rden", [128, 2], F32))

    @block.gpsimd
    def _(g):
        d_g.inc(g.dma_start(out=b1e_r[:, :], in_=b1e_in), 16)
        d_g.inc(g.dma_start(out=b2e_r[:, :], in_=b2e_in), 16)

    def f_rden(v, raw):
        v.reciprocal(out=rden[:, :], in_=den_col[:, :])
        for ec in range(2):
            v.tensor_scalar(out=si_n[:, ec, :], in0=slot_in[:, ec, :],
                            scalar1=rden[:, ec:ec + 1], scalar2=None, op0=ALU.mult)
    vector_section(f_rden)

    @block.tensor
    def _(t):
        W(t, "t", V)
        v_base = V.n
        for ec in range(2):
            if ec == 1:
                W(t, "t", V, v_base + 8)
            for dc in range(NCH):
                mm = t.transpose(pTR[:, dc * 128:(dc + 1) * 128],
                                 si_n[:, ec, dc * 128:(dc + 1) * 128], ident[:, :])
            peT.inc(mm)

    def f_sifm(v, raw):
        base = peT.n - 2
        for ec in range(2):
            W(raw, "v", peT, base + ec + 1)
            for dc in range(NCH):
                v.tensor_copy(out=si_fm[:, dc, ec * 128:(ec + 1) * 128],
                              in_=pTR[:, dc * 128:(dc + 1) * 128])
    vector_section(f_sifm)

    w_after_expert = {}

    @block.sync
    def _(s):
        W(s, "s", d_g)
        for e in range(2):
            wA = w1A if e % 2 == 0 else w1B
            wC = w2A if e % 2 == 0 else w2B
            ws = wsem_for(e)
            ws.inc(s.dma_start(out=wA[:, :, :],
                               in_=w1eb[e * C:(e + 1) * C, :].rearrange("(kc p) h -> p kc h", p=128)), 16)
            ws.inc(s.dma_start(out=wC[:, :, :],
                               in_=w2eb[e * HE:(e + 1) * HE, :].rearrange("(hc p) d -> p hc d", p=128)), 16)
            w_after_expert[e] = ws.n

    for e in range(NE):
        wA = w1A if e % 2 == 0 else w1B
        wC = w2A if e % 2 == 0 else w2B

        @block.tensor
        def _(t, e=e, wA=wA):
            W(t, "t", wsem_for(e), w_after_expert[e])
            W(t, "t", V)
            W(t, "t", A)
            for dc in range(NCH):
                t.matmul(pSA[0:16, 0:HE], si_fm[:, dc, e * 16:(e + 1) * 16],
                         wA[:, dc, :], start=(dc == 0), stop=False)
            peF.inc(t.matmul(pSA[0:16, 0:HE], onesr[0:1, 0:16],
                             b1e_r[0:1, e * HE:(e + 1) * HE], start=False, stop=True))

        def f_gelu_e(sc, raw, e=e):
            if e == 0:
                pin_act(raw, SET_GELU)
            W(raw, "sc", peF)
            W(raw, "sc", d_g)
            sc.activation(out=heb[:, :], in_=pSA[0:16, 0:HE], func=AF.Gelu)
        scalar_section(f_gelu_e)

        @block.tensor
        def _(t):
            W(t, "t", A)
            W(t, "t", V)
            for hc in range(2):
                mm = t.transpose(pTR[:, hc * 16:(hc + 1) * 16],
                                 heb[0:16, hc * 128:(hc + 1) * 128], ident[0:16, 0:16])
            peT.inc(mm)

        def f_heT(v, raw):
            W(raw, "v", peT)
            v.tensor_copy(out=heT[:, :, :].rearrange("p a b -> p (a b)"),
                          in_=pTR[:, 0:32])
        vector_section(f_heT)

        for dvh in range(2):
            @block.tensor
            def _(t, e=e, wC=wC, dvh=dvh):
                W(t, "t", V)
                W(t, "t", A)
                for hc in range(2):
                    t.matmul(pSB[0:16, :], heT[:, hc, :],
                             wC[:, hc, dvh * 512:(dvh + 1) * 512],
                             start=(hc == 0), stop=False)
                peF.inc(t.matmul(pSB[0:16, :], onesr[0:1, 0:16],
                                 b2e_r[0:1, e * C + dvh * 512: e * C + (dvh + 1) * 512],
                                 start=False, stop=True))

            def f_soev(sc, raw, dvh=dvh):
                W(raw, "sc", peF)
                W(raw, "sc", d_sp)
                sc.activation(out=so_st[0:16, dvh * 512:(dvh + 1) * 512],
                              in_=pSB[0:16, :], func=AF.Identity)
            scalar_section(f_soev)

        @block.gpsimd
        def _(g, e=e):
            W(g, "g", A)
            d_sp.inc(g.dma_start(out=sodram[e * 16:(e + 1) * 16, :], in_=so_st[0:16, :]), 16)

        if e + 2 < NE:
            @block.sync
            def _(s, e=e):
                W(s, "s", peF)
                wA2 = w1A if e % 2 == 0 else w1B
                wC2 = w2A if e % 2 == 0 else w2B
                ws = wsem_for(e)
                ws.inc(s.dma_start(out=wA2[:, :, :],
                                   in_=w1eb[(e + 2) * C:(e + 3) * C, :].rearrange("(kc p) h -> p kc h", p=128)), 16)
                ws.inc(s.dma_start(out=wC2[:, :, :],
                                   in_=w2eb[(e + 2) * HE:(e + 3) * HE, :].rearrange("(hc p) d -> p hc d", p=128)), 16)
                w_after_expert[e + 2] = ws.n

    @block.sync
    def _(s):
        W(s, "s", d_sp)
        d_x.inc(s.dma_start(out=so_sb[:, :, :],
                            in_=sodram.rearrange("(ec p) d -> p ec d", p=128)), 16)

    v_b_tile = {}
    v_pO_release = {}
    peo_bt = {}
    a_bepi_end = {}

    def stb(par, col):
        return st8[:, col * 2 + par: col * 2 + par + 1]

    def stbp(col):
        return st8[:, col * 2: col * 2 + 2]

    v_expT = {}

    def emit_bPE(tix):
        par = tix % 2
        pO = pOa if par == 0 else pOb
        dxp = d_xA if par == 0 else d_xB

        @block.gpsimd
        def _(g, tix=tix, par=par, dxp=dxp):
            W(g, "g", d_sp)
            W(g, "g", d_x)
            if tix >= 2:
                W(g, "g", V, v_b_tile[tix - 2])
            dxp.inc(g.dma_start(out=xp_in[:, par, :], in_=xprime[tix * 128:(tix + 1) * 128, :]), 16)
            dxp.inc(g.dma_start(out=exp_in[:, par, :], in_=expb[tix * 128:(tix + 1) * 128, :]), 16)
        dxp_val = dxp.n

        @block.tensor
        def _(t, tix=tix, par=par, dxp=dxp, dxp_val=dxp_val):
            W(t, "t", d_x)
            W(t, "t", dxp, dxp_val)
            W(t, "t", V, v_expT.get(tix - 1) if tix > 0 else V.n)
            for ec in range(2):
                mm = t.transpose(pTR[:, ec * 128:(ec + 1) * 128],
                                 exp_in[:, par, ec * 128:(ec + 1) * 128], ident[:, :])
            peT.inc(mm)

        def f_expT(v, raw):
            W(raw, "v", peT)
            v.tensor_copy(out=expT[:, :, :].rearrange("p a b -> p (a b)"),
                          in_=pTR[:, 0:256])
        vector_section(f_expT)
        v_expT[tix] = V.n

        @block.tensor
        def _(t, tix=tix, pO=pO):
            W(t, "t", V)
            if tix < 2:
                W(t, "t", A)
            if tix >= 2:
                W(t, "t", V, v_pO_release[tix - 2])
            for dvh in range(2):
                sl = slice(dvh * 512, (dvh + 1) * 512)
                for ec in range(2):
                    mm = t.matmul(pO[:, sl], expT[:, ec, :], so_sb[:, ec, sl],
                                  start=(ec == 0), stop=(ec == 1))
            peO.inc(mm)
        peo_bt[tix] = peO.n

    def emit_bEpi_pair(tix):
        def f_bstt(v, raw, tix=tix):
            W(raw, "v", peO, peo_bt[tix + 1])
            W(raw, "v", A, a_bepi_end.get(tix - 2))
            for k in range(2):
                v.scalar_tensor_tensor(out=tC[:, k, :], in0=(pOa if k == 0 else pOb)[:, :],
                                       scalar=rcden[:, tix + k:tix + k + 1],
                                       in1=xp_in[:, k, :],
                                       op0=ALU.mult, op1=ALU.add)
                v.reduce_sum(out=stb(k, S1), in_=tC[:, k, :], axis=AX.X)
        vector_section(f_bstt)
        v_pO_release[tix] = V.n
        v_pO_release[tix + 1] = V.n

        def f_bsq(sc, raw, tix=tix):
            if tix == 0:
                pin_act(raw, SET_LNEXP)
            W(raw, "sc", V)
            sc.activation(out=jC[:, :], in_=tC[:, 0, :], func=AF.Square,
                          accum_out=stb(0, S2))
            sc.activation(out=jC[:, :], in_=tC[:, 1, :], func=AF.Square,
                          accum_out=stb(1, S2))
            sc.activation(out=st8[:, 24:28], in_=st8[:, 0:4], func=AF.Identity)
        scalar_section(f_bsq)

        def fv(v, raw):
            W(raw, "v", A)
            v.tensor_scalar_mul(out=stbp(MEAN), in0=stbp(S1), scalar1=INV_C)
            v.tensor_scalar_mul(out=stbp(S2C), in0=stbp(S2), scalar1=INV_C)
            v.tensor_tensor(out=stbp(10), in0=stbp(MEAN), in1=stbp(MEAN), op=ALU.mult)
            v.tensor_tensor(out=stbp(NVAR), in0=stbp(10), in1=stbp(S2C), op=ALU.subtract)
        vector_section(fv)

        def fs(sc, raw):
            W(raw, "sc", V)
            sc.activation(out=stbp(RSTD), in_=stbp(NVAR),
                          func=AF.Ln, scale=-1.0, bias=eps_c[:, 0:1])
            sc.activation(out=stbp(RSTD), in_=stbp(RSTD), func=AF.Exp, scale=-0.5)
        scalar_section(fs)

        def fv2(v, raw):
            W(raw, "v", A)
            for k in range(2):
                v.tensor_scalar(out=stb(k, NMR), in0=stb(k, MEAN),
                                scalar1=stb(k, RSTD), scalar2=-1.0,
                                op0=ALU.mult, op1=ALU.mult)
        vector_section(fv2)

        def f_bout(sc, raw):
            W(raw, "sc", V)
            W(raw, "sc", d_out)
            for k in range(2):
                sc.activation(out=out_sb[:, k, :], in_=tC[:, k, :], func=AF.Identity,
                              scale=stb(k, RSTD), bias=stb(k, NMR))
        scalar_section(f_bout)
        a_bepi_end[tix] = A.n
        a_bepi_end[tix + 1] = A.n
        v_b_tile[tix] = V.n
        v_b_tile[tix + 1] = V.n

        @block.sync
        def _(s, tix=tix):
            W(s, "s", A)
            for k in range(2):
                d_out.inc(s.dma_start(out=out_d[(tix + k) * 128:(tix + k + 1) * 128, :],
                                      in_=out_sb[:, k, :]), 16)

    for tix in range(0, ntile, 2):
        if tix >= 2:
            emit_bEpi_pair(tix - 2)
        emit_bPE(tix)
        emit_bPE(tix + 1)
    emit_bEpi_pair(ntile - 2)


    @block.sync
    def _(s):
        W(s, "s", d_out, 16 * ntile)

    bctx.close()
    ctx.close()
    return nc


def make_in_maps(inp, t_tok=T):
    x = np.asarray(inp["L0_attn_x"], dtype=np.float32)
    f32 = lambda k: np.ascontiguousarray(np.asarray(inp[k], dtype=np.float32))
    chunked = lambda k: np.ascontiguousarray(
        np.asarray(inp[k], dtype=np.float32).reshape(NCH, 128).T)
    rep = lambda k, n: np.ascontiguousarray(
        np.broadcast_to(np.asarray(inp[k], dtype=np.float32).reshape(1, -1), (128, n)))

    mask = np.zeros((128, NT, BLK), dtype=ml_dtypes.bfloat16)
    ii = np.arange(BLK)[None, None, :]
    jj = np.arange(NT)[None, :, None] * 128 + np.arange(128)[:, None, None]
    mask[:] = (ii >= jj).astype(ml_dtypes.bfloat16)

    common = dict(
        w1q=f32("L0_attn_W1q"), w2q=f32("L0_attn_W2q"),
        w1k=f32("L0_attn_W1k"), w2k=f32("L0_attn_W2k"),
        w1v=f32("L0_attn_W1v"), w2v=f32("L0_attn_W2v"),
        b1q=chunked("L0_attn_b1q"), b2q=chunked("L0_attn_b2q"),
        b1k=chunked("L0_attn_b1k"), b2k=chunked("L0_attn_b2k"),
        b1v=chunked("L0_attn_b1v"),
        b2v=np.ascontiguousarray(np.asarray(inp["L0_attn_b2v"], dtype=np.float32).reshape(1, C)),
        lnw=rep("L0_attn_ln_w", C), lnb=rep("L0_attn_ln_b", C),
        g3=rep("L0_ffn_ln_gamma", C), b3=rep("L0_ffn_ln_beta", C),
        ffnscale=np.ascontiguousarray(
            np.broadcast_to(np.asarray(inp["L0_ffn_scale"], dtype=np.float32).reshape(1, 1), (128, 1))),
        mu=np.ascontiguousarray(np.asarray(inp["L0_ffn_mu"], dtype=np.float32).reshape(C, ES)),
        w1e=f32("L0_ffn_w1"),
        b1e=np.ascontiguousarray(np.asarray(inp["L0_ffn_b1"], dtype=np.float32).reshape(1, NE * HE)),
        w2e=f32("L0_ffn_w2"),
        b2e=np.ascontiguousarray(np.asarray(inp["L0_ffn_b2"], dtype=np.float32).reshape(1, NE * C)),
        mask=mask,
        ident=np.eye(128, dtype=ml_dtypes.bfloat16),
        onesr=np.ones((1, 128), dtype=ml_dtypes.bfloat16),
        onesc=np.ones((128, 1), dtype=ml_dtypes.bfloat16),
    )
    in_maps = []
    for c in range(8):
        m = dict(common)
        m["x"] = np.ascontiguousarray(x[c % 4, :t_tok, :])
        in_maps.append(m)
    return in_maps


def kernel(**inp):
    in_maps = make_in_maps(inp)
    nc = build_nc()
    res = run_bass_kernel_spmd(nc, in_maps, core_ids=list(range(8)))
    out = np.stack([np.asarray(res.results[b]["OUT"]) for b in range(4)], axis=0)
    return out.astype(np.float32)


# version 20
# speedup vs baseline: 1.0164x; 1.0164x over previous
from contextlib import ExitStack

import numpy as np
import ml_dtypes

import concourse.bass as bass
import concourse.mybir as mybir
from concourse.bass_utils import run_bass_kernel_spmd

F32 = mybir.dt.float32
BF16 = mybir.dt.bfloat16
AF = mybir.ActivationFunctionType
ALU = mybir.AluOpType
AX = mybir.AxisListType

T = 8192
C = 1024
NCH = 8
BLK = 512
NBLK = T // BLK
NT = BLK // 128
NTILE = T // 128
ES = 256
NE = 16
HE = 256
ATT_SCALE = 1.0 / float(np.sqrt(np.float32(1024 * 2048)))
EPS = 1e-5
INV_C = 1.0 / C


class Sem:
    __slots__ = ("h", "n", "name")

    def __init__(self, nc, ctx, name):
        self.h = ctx.enter_context(nc.semaphore(name))
        self.n = 0
        self.name = name

    def inc(self, instr, k=1):
        instr.then_inc(self.h, k)
        self.n += k
        return instr


DATA_OPS = {
    "activation", "copy", "mul", "add", "square", "sqrt", "sign",
    "tensor_copy", "tensor_tensor", "tensor_add", "tensor_sub", "tensor_mul",
    "tensor_max", "tensor_scalar", "tensor_single_scalar", "tensor_scalar_add",
    "tensor_scalar_sub", "tensor_scalar_mul", "tensor_scalar_max",
    "tensor_scalar_min", "tensor_relu", "tensor_reduce", "reduce_max",
    "reduce_sum", "scalar_tensor_tensor", "memset", "memzero", "reciprocal",
    "bn_stats", "bn_aggr", "select", "iota",
}


class Chain:

    def __init__(self, eng, sem):
        self._e = eng
        self._s = sem

    def __getattr__(self, name):
        f = getattr(self._e, name)
        if name in DATA_OPS:
            s = self._s
            e = self._e

            def g(*a, **k):
                if s.n > 0:
                    e.wait_ge(s.h, s.n)
                return s.inc(f(*a, **k))

            return g
        return f


def build_nc(nblk=NBLK, debug=False, xnt_pat=True, lag=True, exact_ow=True):
    t_tok = nblk * BLK
    ntile = t_tok // 128
    nc = bass.Bass("TRN2", target_bir_lowering=False)

    x_in = nc.dram_tensor("x", [t_tok, C], F32, kind="ExternalInput").ap()
    wsrcs = [nc.dram_tensor(n, [C, C], F32, kind="ExternalInput").ap()
             for n in ["w1q", "w2q", "w1k", "w2k", "w1v", "w2v"]]
    b1q = nc.dram_tensor("b1q", [128, NCH], F32, kind="ExternalInput").ap()
    b2q = nc.dram_tensor("b2q", [128, NCH], F32, kind="ExternalInput").ap()
    b1k = nc.dram_tensor("b1k", [128, NCH], F32, kind="ExternalInput").ap()
    b2k = nc.dram_tensor("b2k", [128, NCH], F32, kind="ExternalInput").ap()
    b1v = nc.dram_tensor("b1v", [128, NCH], F32, kind="ExternalInput").ap()
    b2v = nc.dram_tensor("b2v", [1, C], F32, kind="ExternalInput").ap()
    lnw = nc.dram_tensor("lnw", [128, C], F32, kind="ExternalInput").ap()
    lnb = nc.dram_tensor("lnb", [128, C], F32, kind="ExternalInput").ap()
    g3 = nc.dram_tensor("g3", [128, C], F32, kind="ExternalInput").ap()
    b3 = nc.dram_tensor("b3", [128, C], F32, kind="ExternalInput").ap()
    ffnscale = nc.dram_tensor("ffnscale", [128, 1], F32, kind="ExternalInput").ap()
    mu_in = nc.dram_tensor("mu", [C, ES], F32, kind="ExternalInput").ap()
    w1e_in = nc.dram_tensor("w1e", [NE, C, HE], F32, kind="ExternalInput").ap()
    b1e_in = nc.dram_tensor("b1e", [1, NE * HE], F32, kind="ExternalInput").ap()
    w2e_in = nc.dram_tensor("w2e", [NE, HE, C], F32, kind="ExternalInput").ap()
    b2e_in = nc.dram_tensor("b2e", [1, NE * C], F32, kind="ExternalInput").ap()
    mask_in = nc.dram_tensor("mask", [128, NT, BLK], BF16, kind="ExternalInput").ap()
    ident_in = nc.dram_tensor("ident", [128, 128], BF16, kind="ExternalInput").ap()
    onesr_in = nc.dram_tensor("onesr", [1, 128], BF16, kind="ExternalInput").ap()
    onesc_in = nc.dram_tensor("onesc", [128, 1], BF16, kind="ExternalInput").ap()
    out_d = nc.dram_tensor("OUT", [t_tok, C], F32, kind="ExternalOutput").ap()

    dbg_kind = "ExternalOutput" if debug else "Internal"
    xb = nc.dram_tensor("xb", [t_tok, C], BF16, kind="Internal").ap()
    wb = [nc.dram_tensor(f"wb{i}", [C, C], BF16, kind="Internal").ap() for i in range(6)]
    xprime = nc.dram_tensor("xprime", [t_tok, C], BF16, kind=dbg_kind).ap()
    expb = nc.dram_tensor("expb", [t_tok, ES], BF16, kind=dbg_kind).ap()
    w1eb = nc.dram_tensor("w1eb", [NE * C, HE], BF16, kind="Internal").ap()
    w2eb = nc.dram_tensor("w2eb", [NE * HE, C], BF16, kind="Internal").ap()
    sodram = nc.dram_tensor("sodram", [ES, C], BF16, kind=dbg_kind).ap()
    dbg_S = nc.dram_tensor("dbg_S", [nblk * 128, NCH * C], BF16, kind=dbg_kind).ap() if debug else None
    dbg_q = nc.dram_tensor("dbg_q", [nblk * 128, NCH * BLK], BF16, kind=dbg_kind).ap() if debug else None
    dbg_at = nc.dram_tensor("dbg_at", [nblk * 128, NT * BLK], BF16, kind=dbg_kind).ap() if debug else None
    dbg_v = nc.dram_tensor("dbg_v", [nblk * 128, NT * C], BF16, kind=dbg_kind).ap() if debug else None
    dbg_x = nc.dram_tensor("dbg_x", [nblk * 128, NT * C], BF16, kind=dbg_kind).ap() if debug else None

    ctx = ExitStack()
    pSA = ctx.enter_context(nc.psum_tensor("pSA", [128, 512], F32))
    pSB = ctx.enter_context(nc.psum_tensor("pSB", [128, 512], F32))
    pAT = ctx.enter_context(nc.psum_tensor("pAT", [128, 512], F32))
    pTR = ctx.enter_context(nc.psum_tensor("pTR", [128, 1024], BF16))
    pOa = ctx.enter_context(nc.psum_tensor("pOa", [128, 1024], F32))
    pOb = ctx.enter_context(nc.psum_tensor("pOb", [128, 1024], F32))

    S32 = ctx.enter_context(nc.sbuf_tensor("S32", [128, NCH, C], F32))
    Sbf = ctx.enter_context(nc.sbuf_tensor("Sbf", [128, NCH, C], BF16))
    slot_in = ctx.enter_context(nc.sbuf_tensor("slot_in", [128, 2, C], F32))
    den_col = ctx.enter_context(nc.sbuf_tensor("den_col", [128, 2], F32))
    rcden = ctx.enter_context(nc.sbuf_tensor("rcden", [128, ntile], F32))
    mu_u = ctx.enter_context(nc.sbuf_tensor("mu_u", [128, NCH, ES], BF16))
    masks = ctx.enter_context(nc.sbuf_tensor("masksS", [128, NT, BLK], BF16))
    ident = ctx.enter_context(nc.sbuf_tensor("identS", [128, 128], BF16))
    onesr = ctx.enter_context(nc.sbuf_tensor("onesrS", [1, 128], BF16))
    onesc = ctx.enter_context(nc.sbuf_tensor("onescS", [128, 1], BF16))
    lnw_b = ctx.enter_context(nc.sbuf_tensor("lnw_b", [128, C], BF16))
    lnb_b = ctx.enter_context(nc.sbuf_tensor("lnb_b", [128, C], BF16))
    g3_b = ctx.enter_context(nc.sbuf_tensor("g3_b", [128, C], BF16))
    b3_b = ctx.enter_context(nc.sbuf_tensor("b3_b", [128, C], BF16))
    sc_bc = ctx.enter_context(nc.sbuf_tensor("sc_bc", [128, 1], F32))
    b1q_s = ctx.enter_context(nc.sbuf_tensor("b1q_s", [128, NCH], F32))
    b2q_s = ctx.enter_context(nc.sbuf_tensor("b2q_s", [128, NCH], F32))
    b1k_s = ctx.enter_context(nc.sbuf_tensor("b1k_s", [128, NCH], F32))
    b2k_s = ctx.enter_context(nc.sbuf_tensor("b2k_s", [128, NCH], F32))
    b1v_s = ctx.enter_context(nc.sbuf_tensor("b1v_s", [128, NCH], F32))
    b2v_r = ctx.enter_context(nc.sbuf_tensor("b2v_r", [1, C], BF16))
    st8 = ctx.enter_context(nc.sbuf_tensor("st8", [128, 32], F32))
    eps_c = ctx.enter_context(nc.sbuf_tensor("eps_c", [128, 1], F32))
    eps24_c = ctx.enter_context(nc.sbuf_tensor("eps24_c", [128, 1], F32))

    d_x = Sem(nc, ctx, "d_x")
    d_c = Sem(nc, ctx, "d_c")
    d_xA = Sem(nc, ctx, "d_xA")
    d_xB = Sem(nc, ctx, "d_xB")
    d_wA = Sem(nc, ctx, "d_wA")
    d_wB = Sem(nc, ctx, "d_wB")
    d_fm = Sem(nc, ctx, "d_fm")
    d_sp = Sem(nc, ctx, "d_sp")
    d_sc = Sem(nc, ctx, "d_sc")
    d_g = Sem(nc, ctx, "d_g")
    d_out = Sem(nc, ctx, "d_out")
    pe1 = Sem(nc, ctx, "pe1")
    peA = Sem(nc, ctx, "peA")
    peT = Sem(nc, ctx, "peT")
    peO = Sem(nc, ctx, "peO")
    peS = Sem(nc, ctx, "peS")
    peF = Sem(nc, ctx, "peF")
    A = Sem(nc, ctx, "A")
    V = Sem(nc, ctx, "V")
    P = Sem(nc, ctx, "P")

    class Waiter:
        def __init__(self):
            self.seen = {}

        def __call__(self, eng, ename, sem, val=None):
            v = sem.n if val is None else val
            if v is None or v <= 0:
                return
            key = (ename, sem.name)
            if self.seen.get(key, -1) >= v:
                return
            self.seen[key] = v
            eng.wait_ge(sem.h, v)

    W = Waiter()
    block = ctx.enter_context(nc.Block())

    SET_LNEXP = 6
    SET_GELU = 10

    def pin_act(raw, set_id):
        raw.add_instruction(mybir.InstLoadActFuncSet(
            name=nc.get_next_instruction_name(), act_func_set_id=set_id,
            ins=[], outs=[]))

    def scalar_section(fn):
        @block.scalar
        def _(sc):
            fn(Chain(sc, A), sc)

    def vector_section(fn):
        @block.vector
        def _(v):
            fn(Chain(v, V), v)

    def pool_section(fn):
        @block.gpsimd
        def _(g):
            fn(Chain(g, P), g)

    wctx = ExitStack()
    wbufA = wctx.enter_context(nc.sbuf_tensor("wbufA", [128, NCH, C], BF16))
    wbufB = wctx.enter_context(nc.sbuf_tensor("wbufB", [128, NCH, C], BF16))
    x_fm = wctx.enter_context(nc.sbuf_tensor("x_fm", [128, NCH, BLK], BF16))
    x_tm = wctx.enter_context(nc.sbuf_tensor("x_tm", [128, NT, C], BF16))
    hbuf = wctx.enter_context(nc.sbuf_tensor("hbuf", [128, NCH, BLK], BF16))
    q_fm = wctx.enter_context(nc.sbuf_tensor("q_fm", [128, NCH, BLK], BF16))
    k_fm = wctx.enter_context(nc.sbuf_tensor("k_fm", [128, NCH, BLK], BF16))
    k_tm = wctx.enter_context(nc.sbuf_tensor("k_tm", [128, NT, C], BF16))
    v_tm = wctx.enter_context(nc.sbuf_tensor("v_tm", [128, NT, C], BF16))
    at_s = wctx.enter_context(nc.sbuf_tensor("at_s", [128, NT, BLK], BF16))
    xp_t = wctx.enter_context(nc.sbuf_tensor("xp_t", [128, 2, C], BF16))
    xn_bf = wctx.enter_context(nc.sbuf_tensor("xn_bf", [128, NT, C], BF16))
    rl2s4 = wctx.enter_context(nc.sbuf_tensor("rl2s4", [128, NT], F32))
    xn_fm = wctx.enter_context(nc.sbuf_tensor("xn_fm", [128, NCH, 128], BF16))
    exp_t = wctx.enter_context(nc.sbuf_tensor("exp_t", [128, NT, ES], BF16))
    tA = wctx.enter_context(nc.sbuf_tensor("tA", [128, 2, C], F32))
    tB = wctx.enter_context(nc.sbuf_tensor("tB", [128, 2, C], F32))
    jb = wctx.enter_context(nc.sbuf_tensor("jb", [128, C], BF16))
    muf = tA[:, :, :].rearrange("p a b -> p (a b)").rearrange("p (kc es) -> p kc es", es=ES)
    musq = hbuf[:, :, 0:ES]
    mrow = tB[0:1, 0, 0:ES]
    mrow2 = jb[0:1, 0:ES]
    mbc = at_s[:, 0, 0:ES]

    S1, S2, MEAN, S2C, NVAR, RSTD, NMR, CDR, RL2 = range(9)

    pe1_after_stage = {}
    stage_evac_end = {}
    v_after_epi = {}
    a_stage_end = {}
    v_after_sadd = {}
    v_xnfm = {}
    v_after_scast = {}
    a_tile_release = {}
    v_epiA_end = {}
    a_xns_ready = {}
    v_ffn_end = {}
    pef_slotin = {}
    pet_xnt = {}

    def emit_stats():
        def fv(v, raw):
            W(raw, "v", A)
            v.tensor_scalar_mul(out=st8[:, MEAN:MEAN + 1], in0=st8[:, S1:S1 + 1], scalar1=INV_C)
            v.tensor_scalar_mul(out=st8[:, S2C:S2C + 1], in0=st8[:, S2:S2 + 1], scalar1=INV_C)
            v.scalar_tensor_tensor(out=st8[:, NVAR:NVAR + 1], in0=st8[:, MEAN:MEAN + 1],
                                   scalar=st8[:, MEAN:MEAN + 1], in1=st8[:, S2C:S2C + 1],
                                   op0=ALU.mult, op1=ALU.subtract)
        vector_section(fv)

        def fs(sc, raw):
            W(raw, "sc", V)
            sc.activation(out=st8[:, RSTD:RSTD + 1], in_=st8[:, NVAR:NVAR + 1],
                          func=AF.Ln, scale=-1.0, bias=eps_c[:, 0:1])
            sc.activation(out=st8[:, RSTD:RSTD + 1], in_=st8[:, RSTD:RSTD + 1],
                          func=AF.Exp, scale=-0.5)
        scalar_section(fs)

        def fv2(v, raw):
            W(raw, "v", A)
            v.tensor_scalar(out=st8[:, NMR:NMR + 1], in0=st8[:, MEAN:MEAN + 1],
                            scalar1=st8[:, RSTD:RSTD + 1], scalar2=-1.0,
                            op0=ALU.mult, op1=ALU.mult)
        vector_section(fv2)

    def f_pro_g(g, raw):
        d_g.inc(raw.dma_start(out=xb, in_=x_in), 16)
        for i in range(6):
            d_g.inc(raw.dma_start(out=wb[i], in_=wsrcs[i]), 16)
        d_g.inc(raw.dma_start(out=lnw_b[:, :], in_=lnw), 16)
        d_g.inc(raw.dma_start(out=lnb_b[:, :], in_=lnb), 16)
        d_g.inc(raw.dma_start(out=g3_b[:, :], in_=g3), 16)
        d_g.inc(raw.dma_start(out=b3_b[:, :], in_=b3), 16)
        d_g.inc(raw.dma_start(out=b2v_r[:, :], in_=b2v), 16)
        g.memset(eps_c[:, :], EPS)
        g.memset(eps24_c[:, :], 1e-24)
        g.memset(S32[:, :, :], 0.0)
        g.memset(Sbf[:, :, :], 0.0)
        g.memset(slot_in[:, :, :], 0.0)
        g.memset(den_col[:, :], 0.0)
        raw.wait_ge(d_g.h, d_g.n)
        d_g.inc(raw.dma_start(out=w1eb, in_=w1e_in.rearrange("e d h -> (e d) h")), 16)
        d_g.inc(raw.dma_start(out=w2eb, in_=w2e_in.rearrange("e h d -> (e h) d")), 16)
    pool_section(f_pro_g)
    d_g_main = d_g.n - 32
    p_memsets = P.n

    @block.sync
    def _(s):
        d_c.inc(s.dma_start(out=masks[:, :, :], in_=mask_in), 16)
        d_c.inc(s.dma_start(out=ident[:, :], in_=ident_in), 16)
        d_c.inc(s.dma_start(out=onesr[:, :], in_=onesr_in), 16)
        d_c.inc(s.dma_start(out=onesc[:, :], in_=onesc_in), 16)
        d_c.inc(s.dma_start(out=sc_bc[:, :], in_=ffnscale), 16)
        d_c.inc(s.dma_start(out=b1q_s[:, :], in_=b1q), 16)
        d_c.inc(s.dma_start(out=b2q_s[:, :], in_=b2q), 16)
        d_c.inc(s.dma_start(out=b1k_s[:, :], in_=b1k), 16)
        d_c.inc(s.dma_start(out=b2k_s[:, :], in_=b2k), 16)
        d_c.inc(s.dma_start(out=b1v_s[:, :], in_=b1v), 16)
        d_c.inc(s.dma_start(out=muf[:, :, :],
                            in_=mu_in.rearrange("(kc p) es -> p kc es", p=128)), 16)

    def f_b2q(v, raw):
        W(raw, "v", d_c)
        v.tensor_scalar_mul(out=b2q_s[:, :], in0=b2q_s[:, :], scalar1=ATT_SCALE)
    vector_section(f_b2q)
    v_b2q = V.n

    def f_musq(sc, raw):
        pin_act(raw, SET_LNEXP)
        W(raw, "sc", d_c)
        sc.activation(out=musq[:, :, :], in_=muf[:, :, :], func=AF.Square)
    scalar_section(f_musq)

    @block.tensor
    def _(t):
        W(t, "t", A)
        W(t, "t", d_c)
        for kc in range(NCH):
            mm = t.matmul(pOa[0:1, 0:ES], onesc[:, 0:1], musq[:, kc, :],
                          start=(kc == 0), stop=(kc == NCH - 1))
        peF.inc(mm)

    def f_mmax(v, raw):
        W(raw, "v", peF)
        v.tensor_scalar_max(out=mrow[:, :], in0=pOa[0:1, 0:ES], scalar1=1e-24)
    vector_section(f_mmax)

    def f_mrstd(sc, raw):
        W(raw, "sc", V)
        sc.activation(out=mrow[:, :], in_=mrow[:, :], func=AF.Ln)
        sc.activation(out=mrow2[:, :], in_=mrow[:, :], func=AF.Exp, scale=-0.5)
    scalar_section(f_mrstd)

    @block.tensor
    def _(t):
        W(t, "t", A)
        peF.inc(t.matmul(pOb[:, 0:ES], onesr[0:1, :], mrow2[0:1, :], start=True, stop=True))

    def f_mbc(v, raw):
        W(raw, "v", peF)
        v.tensor_copy(out=mbc[:, :], in_=pOb[:, 0:ES])
    vector_section(f_mbc)
    v_prologue = V.n

    def f_muu(g, raw):
        W(raw, "g", V)
        g.tensor_tensor(out=mu_u[:, :, :], in0=muf[:, :, :],
                        in1=mbc[:, :].unsqueeze(1).broadcast_to([128, NCH, ES]),
                        op=ALU.mult)
    pool_section(f_muu)
    p_muu = P.n
    pef_prologue = peF.n

    def wbuf_for(G):
        return wbufA if G % 2 == 0 else wbufB

    def wsem_for(G):
        return d_wA if G % 2 == 0 else d_wB

    @block.sync
    def _(s):
        W(s, "s", d_g, d_g_main)
        for G in range(2):
            wsem_for(G).inc(s.dma_start(out=wbuf_for(G)[:, :, :],
                                        in_=wb[G % 6].rearrange("(kc p) n -> p kc n", p=128)), 16)

    def emit_xtm(bb):
        r0 = bb * BLK

        @block.sync
        def _(s, bb=bb, r0=r0):
            if bb > 0:
                W(s, "s", V, v_epiA_end[(bb - 1) * NT + NT - 1])
            else:
                W(s, "s", d_g, d_g_main)
            d_x.inc(s.dma_start(out=x_tm[:, :, :],
                                in_=xb[r0:r0 + BLK, :].rearrange("(ti p) d -> p ti d", p=128)), 16)

    def emit_xfm(bb):
        r0 = bb * BLK

        @block.sync
        def _(s, bb=bb, r0=r0):
            if bb > 0:
                W(s, "s", pe1, pe1_after_stage[6 * (bb - 1) + 4])
            else:
                W(s, "s", d_g, d_g_main)
            for dc in range(NCH):
                d_fm.inc(s.dma_start_transpose(x_fm[:, dc, :],
                                               xb[r0:r0 + BLK, dc * 128:(dc + 1) * 128]), 16)

    evac_pos = {}

    def emit_stage(bb, s_idx, part=None):
        parts = (0, 1) if part is None else (part,)
        for pt in parts:
            G = 6 * bb + s_idx
            wbuf = wbuf_for(G)
            src = x_fm if s_idx in (0, 2, 4) else hbuf
            if s_idx == 0:
                dst, func, bias, scl = hbuf, AF.Gelu, b1q_s, 1.0
            elif s_idx == 1:
                dst, func, bias, scl = q_fm, AF.Identity, b2q_s, ATT_SCALE
            elif s_idx == 2:
                dst, func, bias, scl = hbuf, AF.Gelu, b1k_s, 1.0
            elif s_idx == 3:
                dst, func, bias, scl = k_fm, AF.Identity, b2k_s, 1.0
            elif s_idx == 4:
                dst, func, bias, scl = hbuf, AF.Gelu, b1v_s, 1.0
            else:
                dst, func, bias, scl = v_tm, AF.Identity, None, 1.0
            lo, hi = 4 * pt, 4 * pt + 4
            for m in range(lo, hi):
                evac_pos[(G, m)] = A.n + (m - lo) + 1

            @block.tensor
            def _(t, G=G, s_idx=s_idx, wbuf=wbuf, src=src, bb=bb, lo=lo, hi=hi):
                W(t, "t", wsem_for(G), 16 * (G // 2 + 1))
                if lo == 0:
                    W(t, "t", d_fm)
                    W(t, "t", d_c)
                    W(t, "t", V, v_after_sadd[bb - 1] if bb > 0 else None)
                for m in range(lo, hi):
                    if m < 2:
                        if s_idx in (1, 3, 5):
                            W(t, "t", A, evac_pos[(G - 1, 7)])
                        elif s_idx > 0:
                            W(t, "t", A, evac_pos[(G - 1, 6 + m)])
                    else:
                        W(t, "t", A, evac_pos[(G, m - 2)])
                    p = pSA if m % 2 == 0 else pSB
                    if s_idx < 5:
                        for kc in range(NCH):
                            mm = t.matmul(p[:, :], wbuf[:, kc, m * 128:(m + 1) * 128],
                                          src[:, kc, :], start=(kc == 0), stop=(kc == NCH - 1))
                    else:
                        ti, dvh = divmod(m, 2)
                        for kc in range(NCH):
                            t.matmul(p[:, :], src[:, kc, ti * 128:(ti + 1) * 128],
                                     wbuf[:, kc, dvh * 512:(dvh + 1) * 512],
                                     start=(kc == 0), stop=False)
                        mm = t.matmul(p[:, :], onesr[0:1, :],
                                      b2v_r[0:1, dvh * 512:(dvh + 1) * 512],
                                      start=False, stop=True)
                    pe1.inc(mm)
                if hi == 8:
                    pe1_after_stage[G] = pe1.n

            if hi == 8 and G + 2 < 6 * nblk:
                @block.sync
                def _(s, G=G):
                    W(s, "s", pe1, pe1_after_stage[G])
                    wsem_for(G).inc(s.dma_start(out=wbuf_for(G + 2)[:, :, :],
                                                in_=wb[(G + 2) % 6].rearrange("(kc p) n -> p kc n", p=128)), 16)

            def f_evac(sc, raw, G=G, s_idx=s_idx, dst=dst, func=func, bias=bias,
                       scl=scl, lo=lo, hi=hi):
                W(raw, "sc", V, v_b2q)
                if G == 0:
                    W(raw, "sc", peF, pef_prologue)
                base = pe1.n - (hi - lo)
                for m in range(lo, hi):
                    W(raw, "sc", pe1, base + (m - lo) + 1)
                    p = pSA if m % 2 == 0 else pSB
                    if s_idx < 5:
                        kw = {}
                        if bias is not None:
                            kw["bias"] = bias[:, m:m + 1]
                        if scl != 1.0:
                            kw["scale"] = scl
                        sc.activation(out=dst[:, m, :], in_=p[:, :], func=func, **kw)
                    else:
                        ti, dvh = divmod(m, 2)
                        sc.activation(out=dst[:, ti, dvh * 512:(dvh + 1) * 512],
                                      in_=p[:, :], func=AF.Identity)
            scalar_section(f_evac)
            assert A.n == evac_pos[(G, hi - 1)], (G, lo, A.n, evac_pos[(G, hi - 1)])
            if hi == 8:
                stage_evac_end[G] = A.n
                if s_idx == 5:
                    a_stage_end[bb] = A.n

    act_state = ["L"]

    def emit_pin_sec(set_id):
        @block.scalar
        def _(sc):
            pin_act(sc, set_id)

    def EMIT(cls, fn):
        if cls == "G" and act_state[0] != "G":
            emit_pin_sec(SET_GELU)
            act_state[0] = "G"
        elif cls == "L" and act_state[0] != "L":
            emit_pin_sec(SET_LNEXP)
            act_state[0] = "L"
        fn()

    def stage_cls(s_idx):
        return "G" if s_idx in (0, 2, 4) else "N"

    emit_xtm(0)
    emit_xfm(0)
    for _s in range(6):
        EMIT(stage_cls(_s), lambda s=_s: emit_stage(0, s))

    v_atev = {}
    v_kevac = {}
    peo_tile = {}

    def emit_kT(bb):
        @block.tensor
        def _(t, bb=bb):
            W(t, "t", A)
            W(t, "t", V)
            v_base = V.n
            for jt in range(NT):
                if jt >= 1:
                    W(t, "t", V, v_base + jt)
                for dc in range(NCH):
                    mm = t.transpose(pTR[:, dc * 128:(dc + 1) * 128],
                                     k_fm[:, dc, jt * 128:(jt + 1) * 128], ident[:, :])
                peT.inc(mm)

        def f_kevac(v, raw, bb=bb):
            base = peT.n - NT
            for jt in range(NT):
                W(raw, "v", peT, base + jt + 1)
                v.tensor_copy(out=k_tm[:, jt, :], in_=pTR[:, :])
        vector_section(f_kevac)
        v_kevac[bb] = V.n

    def emit_AT(bb):
        @block.tensor
        def _(t, bb=bb):
            W(t, "t", V)
            v_base = V.n
            for jt in range(NT):
                if jt >= 1:
                    W(t, "t", V, v_base + jt)
                for kc in range(NCH):
                    mm = t.matmul(pAT[:, :], k_fm[:, kc, jt * 128:(jt + 1) * 128],
                                  q_fm[:, kc, :], start=(kc == 0), stop=(kc == NCH - 1))
                peA.inc(mm)

        def f_atevac(v, raw, bb=bb):
            base = peA.n - NT
            W(raw, "v", d_c)
            W(raw, "v", P, p_muu)
            for jt in range(NT):
                W(raw, "v", peA, base + jt + 1)
                v.tensor_tensor(out=at_s[:, jt, :], in0=pAT[:, :],
                                in1=masks[:, jt, :], op=ALU.mult)
        vector_section(f_atevac)
        v_atev[bb] = V.n

    def supd_segs(bb):
        segs = []
        supd_banks = [pSA[:, :], pSB[:, :]]
        v_supd = {}
        st_ctr = {}

        def one(g_i):
            dk, dvh = divmod(g_i, 2)
            if g_i >= 2:
                ga = g_i - 2

                def f_sadd(v, raw, ga=ga, bb=bb):
                    if ga == 0:
                        W(raw, "v", P, p_memsets)
                        W(raw, "v", d_sc, v_after_scast.get(bb - 1))
                    W(raw, "v", peS, st_ctr[ga])
                    da, dva = divmod(ga, 2)
                    v.tensor_tensor(out=S32[:, da, dva * 512:(dva + 1) * 512],
                                    in0=supd_banks[ga % 2],
                                    in1=S32[:, da, dva * 512:(dva + 1) * 512], op=ALU.add)
                vector_section(f_sadd)
                v_supd[ga] = V.n
            if g_i < NCH * 2:
                @block.tensor
                def _(t, g_i=g_i, dk=dk, dvh=dvh, bb=bb):
                    if g_i == 0:
                        W(t, "t", A, a_stage_end[bb])
                        W(t, "t", V, v_kevac[bb])
                        W(t, "t", P, p_memsets)
                    if g_i >= 2:
                        W(t, "t", V, v_supd[g_i - 2])
                    p = supd_banks[g_i % 2]
                    for jt in range(NT):
                        mm = t.matmul(p, k_tm[:, jt, dk * 128:(dk + 1) * 128],
                                      v_tm[:, jt, dvh * 512:(dvh + 1) * 512],
                                      start=(jt == 0), stop=(jt == NT - 1))
                    peS.inc(mm)
                st_ctr[g_i] = peS.n
            if g_i == NCH * 2 + 1:
                v_after_sadd[bb] = V.n
        for g_i in range(NCH * 2 + 2):
            segs.append(lambda g=g_i: one(g))
        return segs

    emit_kT(0)
    emit_AT(0)
    for _seg in supd_segs(0):
        _seg()

    for b in range(nblk):
        def st(par, col):
            return st8[:, col * 2 + par: col * 2 + par + 1]

        def stp(col):
            return st8[:, col * 2: col * 2 + 2]

        def emit_stats2(par):
            def fv(v, raw, par=par):
                W(raw, "v", A)
                v.tensor_scalar_mul(out=st(par, MEAN), in0=st(par, S1), scalar1=INV_C)
                v.tensor_scalar_mul(out=st(par, S2C), in0=st(par, S2), scalar1=INV_C)
                v.scalar_tensor_tensor(out=st(par, NVAR), in0=st(par, MEAN),
                                       scalar=st(par, MEAN), in1=st(par, S2C),
                                       op0=ALU.mult, op1=ALU.subtract)
            vector_section(fv)

            def fs(sc, raw, par=par):
                W(raw, "sc", V)
                sc.activation(out=st(par, RSTD), in_=st(par, NVAR),
                              func=AF.Ln, scale=-1.0, bias=eps_c[:, 0:1])
                sc.activation(out=st(par, RSTD), in_=st(par, RSTD),
                              func=AF.Exp, scale=-0.5)
            scalar_section(fs)

            def fv2(v, raw, par=par):
                W(raw, "v", A)
                v.tensor_scalar(out=st(par, NMR), in0=st(par, MEAN),
                                scalar1=st(par, RSTD), scalar2=-1.0,
                                op0=ALU.mult, op1=ALU.mult)
            vector_section(fv2)

        def emit_O(ti, b=b, v_at_evac=None):
            tile64 = b * NT + ti
            par = ti % 2
            pO = pOa if par == 0 else pOb

            @block.tensor
            def _(t, b=b, ti=ti, pO=pO, tile64=tile64, v_at_evac=v_at_evac):
                W(t, "t", V, v_at_evac if exact_ow else None)
                W(t, "t", V, v_ffn_end.get(b * NT - 1))
                W(t, "t", d_sc, v_after_scast.get(b - 1))
                W(t, "t", P, p_memsets)
                W(t, "t", P, p_muu)
                if tile64 >= 2:
                    W(t, "t", A, a_tile_release[tile64 - 2])
                for dvh in range(2):
                    sl = slice(dvh * 512, (dvh + 1) * 512)
                    for jt in range(NT):
                        t.matmul(pO[:, sl], at_s[:, jt, ti * 128:(ti + 1) * 128],
                                 v_tm[:, jt, sl], start=(jt == 0), stop=False)
                    for kc in range(NCH):
                        mm = t.matmul(pO[:, sl], q_fm[:, kc, ti * 128:(ti + 1) * 128],
                                      Sbf[:, kc, sl], start=False, stop=(kc == NCH - 1))
                peO.inc(mm)
            peo_tile[tile64] = peO.n

        def epiA_pair_segs(t0, b=b):
            segs = []
            ta, tb_ = t0, t0 + 1
            t64a, t64b = b * NT + ta, b * NT + tb_

            def f_e1a(sc, raw):
                W(raw, "sc", peO, peo_tile[t64a])
                W(raw, "sc", P, p_muu)
                W(raw, "sc", peF, pef_prologue)
                W(raw, "sc", V, v_epiA_end.get(t64a - 2))
                sc.activation(out=tA[:, 0, :], in_=pOa[:, :], func=AF.Copy,
                              accum_out=st(0, S1))
                sc.activation(out=jb[:, :], in_=pOa[:, :], func=AF.Square,
                              accum_out=st(0, S2))
            def s_e1a():
                scalar_section(f_e1a)
                a_tile_release[t64a] = A.n
            segs.append(s_e1a)

            def f_e1b(sc, raw):
                W(raw, "sc", peO, peo_tile[t64b])
                W(raw, "sc", V, v_epiA_end.get(t64b - 2))
                sc.activation(out=tA[:, 1, :], in_=pOb[:, :], func=AF.Copy,
                              accum_out=st(1, S1))
                sc.activation(out=jb[:, :], in_=pOb[:, :], func=AF.Square,
                              accum_out=st(1, S2))
                sc.activation(out=stp(MEAN), in_=stp(S1), func=AF.Identity,
                              scale=-INV_C)
                sc.activation(out=stp(S2C), in_=stp(S2), func=AF.Identity,
                              scale=INV_C)
            def s_e1b():
                scalar_section(f_e1b)
                a_tile_release[t64b] = A.n
            segs.append(s_e1b)

            def f_nv1(v, raw):
                W(raw, "v", A)
                v.tensor_tensor(out=stp(10), in0=stp(MEAN), in1=stp(MEAN), op=ALU.mult)
                v.tensor_tensor(out=stp(NVAR), in0=stp(10), in1=stp(S2C), op=ALU.subtract)
            segs.append(lambda: vector_section(f_nv1))

            def f_a1(sc, raw):
                W(raw, "sc", V)
                sc.activation(out=stp(RSTD), in_=stp(NVAR),
                              func=AF.Ln, scale=-1.0, bias=eps_c[:, 0:1])
                sc.activation(out=stp(RSTD), in_=stp(RSTD), func=AF.Exp, scale=-0.5)
                sc.activation(out=st(0, NMR), in_=st(0, MEAN), func=AF.Identity,
                              scale=st(0, RSTD))
                sc.activation(out=st(1, NMR), in_=st(1, MEAN), func=AF.Identity,
                              scale=st(1, RSTD))
                sc.activation(out=tB[:, 0, :], in_=tA[:, 0, :], func=AF.Identity,
                              scale=st(0, RSTD), bias=st(0, NMR))
                sc.activation(out=tB[:, 1, :], in_=tA[:, 1, :], func=AF.Identity,
                              scale=st(1, RSTD), bias=st(1, NMR))
            segs.append(lambda: scalar_section(f_a1))

            lnw_bc = lnw_b[:, :].unsqueeze(1).broadcast_to([128, 2, C])
            lnb_bc = lnb_b[:, :].unsqueeze(1).broadcast_to([128, 2, C])
            g3_bc = g3_b[:, :].unsqueeze(1).broadcast_to([128, 2, C])
            b3_bc = b3_b[:, :].unsqueeze(1).broadcast_to([128, 2, C])

            def f_e34(v, raw):
                W(raw, "v", A)
                W(raw, "v", peF, pef_prologue)
                W(raw, "v", d_x)
                v.tensor_tensor(out=tA[:, :, :], in0=tB[:, :, :], in1=lnw_bc, op=ALU.mult)
                v.tensor_tensor(out=tB[:, :, :], in0=tA[:, :, :], in1=lnb_bc, op=ALU.add)
                v.tensor_tensor(out=tA[:, :, :], in0=tB[:, :, :],
                                in1=x_tm[:, t0:t0 + 2, :], op=ALU.add)
                v.reduce_sum(out=st(0, S1), in_=tA[:, 0, :], axis=AX.X)
                v.reduce_sum(out=st(1, S1), in_=tA[:, 1, :], axis=AX.X)
            segs.append(lambda: vector_section(f_e34))

            def f_e5(sc, raw):
                W(raw, "sc", V)
                sc.activation(out=jb[:, :], in_=tA[:, 0, :], func=AF.Square,
                              accum_out=st(0, S2))
                sc.activation(out=jb[:, :], in_=tA[:, 1, :], func=AF.Square,
                              accum_out=st(1, S2))
                sc.activation(out=stp(MEAN), in_=stp(S1), func=AF.Identity,
                              scale=-INV_C)
                sc.activation(out=stp(S2C), in_=stp(S2), func=AF.Identity,
                              scale=INV_C)
            segs.append(lambda: scalar_section(f_e5))

            def f_nv2(v, raw):
                W(raw, "v", A)
                v.tensor_tensor(out=stp(10), in0=stp(MEAN), in1=stp(MEAN), op=ALU.mult)
                v.tensor_tensor(out=stp(NVAR), in0=stp(10), in1=stp(S2C), op=ALU.subtract)
            segs.append(lambda: vector_section(f_nv2))

            def f_a2(sc, raw):
                W(raw, "sc", V)
                W(raw, "sc", d_sp)
                sc.activation(out=stp(RSTD), in_=stp(NVAR),
                              func=AF.Ln, scale=-1.0, bias=eps_c[:, 0:1])
                sc.activation(out=stp(RSTD), in_=stp(RSTD), func=AF.Exp, scale=-0.5)
                sc.activation(out=st(0, NMR), in_=st(0, MEAN), func=AF.Identity,
                              scale=st(0, RSTD))
                sc.activation(out=st(1, NMR), in_=st(1, MEAN), func=AF.Identity,
                              scale=st(1, RSTD))
                sc.activation(out=xp_t[:, 0, :], in_=tA[:, 0, :], func=AF.Identity,
                              scale=st(0, RSTD), bias=st(0, NMR),
                              accum_out=st(0, S1))
                sc.activation(out=jb[:, :], in_=xp_t[:, 0, :], func=AF.Square,
                              accum_out=st(0, S2))
                sc.activation(out=xp_t[:, 1, :], in_=tA[:, 1, :], func=AF.Identity,
                              scale=st(1, RSTD), bias=st(1, NMR),
                              accum_out=st(1, S1))
                sc.activation(out=jb[:, :], in_=xp_t[:, 1, :], func=AF.Square,
                              accum_out=st(1, S2))
                sc.activation(out=stp(MEAN), in_=stp(S1), func=AF.Identity,
                              scale=-INV_C)
                sc.activation(out=stp(S2C), in_=stp(S2), func=AF.Identity,
                              scale=INV_C)
            segs.append(lambda: scalar_section(f_a2))

            def s_spill(ti):
                @block.gpsimd
                def _(g, ti=ti):
                    W(g, "g", A)
                    d_sp.inc(g.dma_start(out=xprime[b * BLK + ti * 128: b * BLK + (ti + 1) * 128, :],
                                         in_=xp_t[:, ti % 2, :]), 16)
            segs.append(lambda: s_spill(ta))
            segs.append(lambda: s_spill(tb_))

            def f_nv3(v, raw):
                W(raw, "v", A)
                v.tensor_tensor(out=stp(10), in0=stp(MEAN), in1=stp(MEAN), op=ALU.mult)
                v.tensor_tensor(out=stp(NVAR), in0=stp(10), in1=stp(S2C), op=ALU.subtract)
            segs.append(lambda: vector_section(f_nv3))

            def f_a3(sc, raw):
                W(raw, "sc", V)
                sc.activation(out=stp(RSTD), in_=stp(NVAR),
                              func=AF.Ln, scale=-1.0, bias=eps_c[:, 0:1])
                sc.activation(out=stp(RSTD), in_=stp(RSTD), func=AF.Exp, scale=-0.5)
                sc.activation(out=st(0, NMR), in_=st(0, MEAN), func=AF.Identity,
                              scale=st(0, RSTD))
                sc.activation(out=st(1, NMR), in_=st(1, MEAN), func=AF.Identity,
                              scale=st(1, RSTD))
                sc.activation(out=tA[:, 0, :], in_=xp_t[:, 0, :], func=AF.Identity,
                              scale=st(0, RSTD), bias=st(0, NMR))
                sc.activation(out=tA[:, 1, :], in_=xp_t[:, 1, :], func=AF.Identity,
                              scale=st(1, RSTD), bias=st(1, NMR))
            segs.append(lambda: scalar_section(f_a3))

            def f_e9(v, raw):
                W(raw, "v", A)
                W(raw, "v", peF, pef_slotin.get(t64a - 4))
                W(raw, "v", peF, pef_slotin.get(t64b - 4))
                W(raw, "v", peT, pet_xnt.get(t64a - 4))
                W(raw, "v", peT, pet_xnt.get(t64b - 4))
                v.tensor_tensor(out=tB[:, :, :], in0=tA[:, :, :], in1=g3_bc, op=ALU.mult)
                v.tensor_tensor(out=xn_bf[:, t0:t0 + 2, :], in0=tB[:, :, :],
                                in1=b3_bc, op=ALU.add)
            segs.append(lambda: vector_section(f_e9))

            def f_a4(sc, raw):
                W(raw, "sc", V)
                sc.activation(out=jb[:, :], in_=xn_bf[:, ta, :], func=AF.Square,
                              accum_out=st(0, S2))
                sc.activation(out=jb[:, :], in_=xn_bf[:, tb_, :], func=AF.Square,
                              accum_out=st(1, S2))
                sc.activation(out=stp(RL2), in_=stp(S2), func=AF.Ln,
                              bias=eps24_c[:, 0:1])
                sc.activation(out=stp(RL2), in_=stp(RL2), func=AF.Exp, scale=-0.5)
                sc.activation(out=rl2s4[:, t0:t0 + 2], in_=stp(RL2), func=AF.Identity,
                              scale=sc_bc[:, 0:1])
            def s_a4():
                scalar_section(f_a4)
                a_xns_ready[t64a] = A.n
                a_xns_ready[t64b] = A.n
                v_epiA_end[t64a] = V.n
                v_epiA_end[t64b] = V.n
            segs.append(s_a4)
            return segs

        def emit_ffn(ti, b=b):
            tile64 = b * NT + ti
            par = ti % 2
            cdr_c = CDR if ti < 2 else 9
            pX = pTR[:, :].bitcast(F32)
            pDen = pTR[:, 512:516].bitcast(F32)
            pATb = pAT[:, :].bitcast(BF16)

            @block.tensor
            def _(t, par=par, tile64=tile64, ti=ti, pATb=pATb):
                W(t, "t", A, a_xns_ready[tile64])
                if ti == 0:
                    W(t, "t", V, v_atev[b])
                    W(t, "t", V, v_ffn_end.get(tile64 - 1))
                elif ti == 2:
                    if (b + 1) in v_atev:
                        W(t, "t", V, v_atev[b + 1])
                    else:
                        W(t, "t", V, v_xnfm[tile64 - 1])
                else:
                    W(t, "t", V, v_xnfm[tile64 - 1])
                for dc in range(NCH):
                    mm = t.transpose(pATb[:, dc * 128:(dc + 1) * 128],
                                     xn_bf[:, ti, dc * 128:(dc + 1) * 128], ident[:, :])
                peT.inc(mm)
            pet_xnt[tile64] = peT.n

            def f_xnfm(v, raw, pATb=pATb):
                W(raw, "v", peT)
                v.tensor_copy(out=xn_fm[:, :, :].rearrange("p a b -> p (a b)"),
                              in_=pATb[:, :])
            vector_section(f_xnfm)
            v_xnfm[tile64] = V.n

            @block.tensor
            def _(t, pX=pX, tile64=tile64, ti=ti):
                W(t, "t", V, v_xnfm[tile64])
                if ti in (1, 2, 3):
                    W(t, "t", V, v_ffn_end.get(tile64 - 1))
                W(t, "t", P, p_muu)
                for dc in range(NCH):
                    mm = t.matmul(pX[:, 0:ES], xn_fm[:, dc, :], mu_u[:, dc, :],
                                  start=(dc == 0), stop=(dc == NCH - 1))
                peF.inc(mm)

            def f_exp(sc, raw, par=par, tile64=tile64, pX=pX, ti=ti, cdr_c=cdr_c):
                W(raw, "sc", peF)
                W(raw, "sc", d_sp)
                W(raw, "sc", V, v_ffn_end.get(tile64 - 4))
                sc.activation(out=exp_t[:, ti, :], in_=pX[:, 0:ES], func=AF.Exp,
                              scale=rl2s4[:, ti:ti + 1],
                              accum_out=st(par, cdr_c))
                sc.activation(out=st(par, 14), in_=st(par, cdr_c), func=AF.Identity)
            scalar_section(f_exp)

            def f_rcd(v, raw, tile64=tile64, par=par, cdr_c=cdr_c):
                W(raw, "v", A)
                v.reciprocal(out=rcden[:, tile64:tile64 + 1], in_=st(par, cdr_c))
            vector_section(f_rcd)

            @block.gpsimd
            def _(g, b=b, ti=ti, par=par):
                W(g, "g", A)
                d_sp.inc(g.dma_start(out=expb[b * BLK + ti * 128: b * BLK + (ti + 1) * 128, :],
                                     in_=exp_t[:, ti, :]), 16)

            @block.tensor
            def _(t, par=par, ti=ti, pDen=pDen):
                W(t, "t", A)
                W(t, "t", V)
                for ec in range(2):
                    mm = t.matmul(pDen[:, ec:ec + 1], exp_t[:, ti, ec * 128:(ec + 1) * 128],
                                  onesc[:, 0:1], start=True, stop=True)
                peF.inc(mm)

            def f_den(v, raw, pDen=pDen):
                W(raw, "v", peF)
                W(raw, "v", P, p_memsets)
                v.tensor_tensor(out=den_col[:, :], in0=pDen[:, 0:2],
                                in1=den_col[:, :], op=ALU.add)
            vector_section(f_den)

            for ec in range(2):
                for nh in range(2):
                    @block.tensor
                    def _(t, ec=ec, nh=nh, par=par, ti=ti, pX=pX):
                        W(t, "t", V)
                        W(t, "t", A)
                        peF.inc(t.matmul(pX[:, :], exp_t[:, ti, ec * 128:(ec + 1) * 128],
                                         xn_bf[:, ti, nh * 512:(nh + 1) * 512],
                                         start=True, stop=True))

                    def f_sl(v, raw, ec=ec, nh=nh, pX=pX):
                        W(raw, "v", peF)
                        v.tensor_tensor(
                            out=slot_in[:, ec, nh * 512:(nh + 1) * 512], in0=pX[:, :],
                            in1=slot_in[:, ec, nh * 512:(nh + 1) * 512], op=ALU.add)
                    vector_section(f_sl)
            pef_slotin[tile64] = peF.n
            v_ffn_end[tile64] = V.n

        CCLS = ["N", "N", "N", "L", "N", "N", "N", "L", "N", "N", "N", "L", "N", "L"]

        def crun(c, lo, hi):
            for k in range(lo, hi):
                EMIT(CCLS[k], c[k])

        def scast_dma(b=b):
            @block.gpsimd
            def _(g, b=b):
                W(g, "g", peO, peo_tile[b * NT + 3])
                d_sc.inc(g.dma_start(out=Sbf[:, :, :], in_=S32[:, :, :]), 16)
            v_after_scast[b] = d_sc.n

        if b == 0:
            emit_O(0, v_at_evac=v_atev[b])
            emit_O(1, v_at_evac=v_atev[b])
        c01 = epiA_pair_segs(0)
        crun(c01, 0, 2)
        emit_O(2, v_at_evac=v_atev[b])
        emit_O(3, v_at_evac=v_atev[b])
        scast_dma(b)


        c23 = epiA_pair_segs(2)
        nxt = b + 1 < nblk
        if nxt:
            emit_xfm(b + 1)
            crun(c01, 2, 3)
            EMIT("G", lambda: emit_stage(b + 1, 0))
            crun(c01, 3, 5)
            EMIT("N", lambda: emit_stage(b + 1, 1, 0))
            crun(c01, 5, 7)
            EMIT("N", lambda: emit_stage(b + 1, 1, 1))
            crun(c01, 7, 9)
            EMIT("G", lambda: emit_stage(b + 1, 2, 0))
            crun(c01, 9, 11)
            EMIT("G", lambda: emit_stage(b + 1, 2, 1))
            crun(c01, 11, 12)
            EMIT("N", lambda: emit_stage(b + 1, 3, 0))
            crun(c01, 12, 13)
            EMIT("N", lambda: emit_stage(b + 1, 3, 1))
            crun(c01, 13, 14)
            EMIT("L", lambda: emit_ffn(0))
            EMIT("L", lambda: emit_ffn(1))
            crun(c23, 0, 3)
            EMIT("G", lambda: emit_stage(b + 1, 4, 0))
            crun(c23, 3, 5)
            EMIT("G", lambda: emit_stage(b + 1, 4, 1))
            crun(c23, 5, 7)
            EMIT("N", lambda: emit_stage(b + 1, 5, 0))
            crun(c23, 7, 9)
            EMIT("N", lambda: emit_stage(b + 1, 5, 1))
            crun(c23, 9, 11)
            emit_kT(b + 1)
            crun(c23, 11, 12)
            emit_AT(b + 1)
            crun(c23, 12, 13)
            emit_O(0, b=b + 1, v_at_evac=v_atev[b + 1])
            crun(c23, 13, 14)
            emit_O(1, b=b + 1, v_at_evac=v_atev[b + 1])
            for s in supd_segs(b + 1):
                s()
            emit_xtm(b + 1)
        else:
            crun(c01, 2, 14)
            EMIT("L", lambda: emit_ffn(0))
            EMIT("L", lambda: emit_ffn(1))
            crun(c23, 0, 14)
        EMIT("L", lambda: emit_ffn(2))
        EMIT("L", lambda: emit_ffn(3))
        v_after_epi[b] = V.n

        if debug:
            @block.gpsimd
            def _(g, b=b):
                W(g, "g", V, v_after_scast[b])
                d_sp.inc(g.dma_start(out=dbg_S[b * 128:(b + 1) * 128, :],
                                     in_=Sbf[:, :, :].rearrange("p a c -> p (a c)")), 16)
                d_sp.inc(g.dma_start(out=dbg_q[b * 128:(b + 1) * 128, :],
                                     in_=q_fm[:, :, :].rearrange("p a c -> p (a c)")), 16)
                d_sp.inc(g.dma_start(out=dbg_at[b * 128:(b + 1) * 128, :],
                                     in_=at_s[:, :, :].rearrange("p a c -> p (a c)")), 16)
                d_sp.inc(g.dma_start(out=dbg_v[b * 128:(b + 1) * 128, :],
                                     in_=v_tm[:, :, :].rearrange("p a c -> p (a c)")), 16)
                d_sp.inc(g.dma_start(out=dbg_x[b * 128:(b + 1) * 128, :],
                                     in_=x_tm[:, :, :].rearrange("p a c -> p (a c)")), 16)

    wctx.close()
    bctx = ExitStack()
    w1A = bctx.enter_context(nc.sbuf_tensor("w1A", [128, NCH, HE], BF16))
    w1B = bctx.enter_context(nc.sbuf_tensor("w1B", [128, NCH, HE], BF16))
    w2A = bctx.enter_context(nc.sbuf_tensor("w2A", [128, 2, C], BF16))
    w2B = bctx.enter_context(nc.sbuf_tensor("w2B", [128, 2, C], BF16))
    b1e_r = bctx.enter_context(nc.sbuf_tensor("b1e_r", [1, NE * HE], BF16))
    b2e_r = bctx.enter_context(nc.sbuf_tensor("b2e_r", [1, NE * C], BF16))
    si_n = bctx.enter_context(nc.sbuf_tensor("si_n", [128, 2, C], BF16))
    si_fm = bctx.enter_context(nc.sbuf_tensor("si_fm", [128, NCH, ES], BF16))
    heb = bctx.enter_context(nc.sbuf_tensor("heb", [16, HE], BF16))
    heT = bctx.enter_context(nc.sbuf_tensor("heT", [128, 2, 16], BF16))
    so_st = bctx.enter_context(nc.sbuf_tensor("so_st", [16, C], BF16))
    so_sb = bctx.enter_context(nc.sbuf_tensor("so_sb", [128, 2, C], BF16))
    expT = bctx.enter_context(nc.sbuf_tensor("expT", [128, 2, 128], BF16))
    xp_in = bctx.enter_context(nc.sbuf_tensor("xp_in", [128, 2, C], BF16))
    exp_in = bctx.enter_context(nc.sbuf_tensor("exp_in", [128, 2, ES], BF16))
    out_sb = bctx.enter_context(nc.sbuf_tensor("out_sb", [128, 2, C], F32))
    tC = bctx.enter_context(nc.sbuf_tensor("tC", [128, 2, C], F32))
    jC = bctx.enter_context(nc.sbuf_tensor("jC", [128, C], BF16))
    rden = bctx.enter_context(nc.sbuf_tensor("rden", [128, 2], F32))

    @block.gpsimd
    def _(g):
        d_g.inc(g.dma_start(out=b1e_r[:, :], in_=b1e_in), 16)
        d_g.inc(g.dma_start(out=b2e_r[:, :], in_=b2e_in), 16)

    def f_rden(v, raw):
        v.reciprocal(out=rden[:, :], in_=den_col[:, :])
        for ec in range(2):
            v.tensor_scalar(out=si_n[:, ec, :], in0=slot_in[:, ec, :],
                            scalar1=rden[:, ec:ec + 1], scalar2=None, op0=ALU.mult)
    vector_section(f_rden)

    @block.tensor
    def _(t):
        W(t, "t", V)
        v_base = V.n
        for ec in range(2):
            if ec == 1:
                W(t, "t", V, v_base + 8)
            for dc in range(NCH):
                mm = t.transpose(pTR[:, dc * 128:(dc + 1) * 128],
                                 si_n[:, ec, dc * 128:(dc + 1) * 128], ident[:, :])
            peT.inc(mm)

    def f_sifm(v, raw):
        base = peT.n - 2
        for ec in range(2):
            W(raw, "v", peT, base + ec + 1)
            for dc in range(NCH):
                v.tensor_copy(out=si_fm[:, dc, ec * 128:(ec + 1) * 128],
                              in_=pTR[:, dc * 128:(dc + 1) * 128])
    vector_section(f_sifm)

    w_after_expert = {}

    @block.sync
    def _(s):
        W(s, "s", d_g)
        for e in range(2):
            wA = w1A if e % 2 == 0 else w1B
            wC = w2A if e % 2 == 0 else w2B
            ws = wsem_for(e)
            ws.inc(s.dma_start(out=wA[:, :, :],
                               in_=w1eb[e * C:(e + 1) * C, :].rearrange("(kc p) h -> p kc h", p=128)), 16)
            ws.inc(s.dma_start(out=wC[:, :, :],
                               in_=w2eb[e * HE:(e + 1) * HE, :].rearrange("(hc p) d -> p hc d", p=128)), 16)
            w_after_expert[e] = ws.n

    for e in range(NE):
        wA = w1A if e % 2 == 0 else w1B
        wC = w2A if e % 2 == 0 else w2B

        @block.tensor
        def _(t, e=e, wA=wA):
            W(t, "t", wsem_for(e), w_after_expert[e])
            W(t, "t", V)
            W(t, "t", A)
            for dc in range(NCH):
                t.matmul(pSA[0:16, 0:HE], si_fm[:, dc, e * 16:(e + 1) * 16],
                         wA[:, dc, :], start=(dc == 0), stop=False)
            peF.inc(t.matmul(pSA[0:16, 0:HE], onesr[0:1, 0:16],
                             b1e_r[0:1, e * HE:(e + 1) * HE], start=False, stop=True))

        def f_gelu_e(sc, raw, e=e):
            if e == 0:
                pin_act(raw, SET_GELU)
            W(raw, "sc", peF)
            W(raw, "sc", d_g)
            sc.activation(out=heb[:, :], in_=pSA[0:16, 0:HE], func=AF.Gelu)
        scalar_section(f_gelu_e)

        @block.tensor
        def _(t):
            W(t, "t", A)
            W(t, "t", V)
            for hc in range(2):
                mm = t.transpose(pTR[:, hc * 16:(hc + 1) * 16],
                                 heb[0:16, hc * 128:(hc + 1) * 128], ident[0:16, 0:16])
            peT.inc(mm)

        def f_heT(v, raw):
            W(raw, "v", peT)
            v.tensor_copy(out=heT[:, :, :].rearrange("p a b -> p (a b)"),
                          in_=pTR[:, 0:32])
        vector_section(f_heT)

        for dvh in range(2):
            @block.tensor
            def _(t, e=e, wC=wC, dvh=dvh):
                W(t, "t", V)
                W(t, "t", A)
                for hc in range(2):
                    t.matmul(pSB[0:16, :], heT[:, hc, :],
                             wC[:, hc, dvh * 512:(dvh + 1) * 512],
                             start=(hc == 0), stop=False)
                peF.inc(t.matmul(pSB[0:16, :], onesr[0:1, 0:16],
                                 b2e_r[0:1, e * C + dvh * 512: e * C + (dvh + 1) * 512],
                                 start=False, stop=True))

            def f_soev(sc, raw, dvh=dvh):
                W(raw, "sc", peF)
                W(raw, "sc", d_sp)
                sc.activation(out=so_st[0:16, dvh * 512:(dvh + 1) * 512],
                              in_=pSB[0:16, :], func=AF.Identity)
            scalar_section(f_soev)

        @block.gpsimd
        def _(g, e=e):
            W(g, "g", A)
            d_sp.inc(g.dma_start(out=sodram[e * 16:(e + 1) * 16, :], in_=so_st[0:16, :]), 16)

        if e + 2 < NE:
            @block.sync
            def _(s, e=e):
                W(s, "s", peF)
                wA2 = w1A if e % 2 == 0 else w1B
                wC2 = w2A if e % 2 == 0 else w2B
                ws = wsem_for(e)
                ws.inc(s.dma_start(out=wA2[:, :, :],
                                   in_=w1eb[(e + 2) * C:(e + 3) * C, :].rearrange("(kc p) h -> p kc h", p=128)), 16)
                ws.inc(s.dma_start(out=wC2[:, :, :],
                                   in_=w2eb[(e + 2) * HE:(e + 3) * HE, :].rearrange("(hc p) d -> p hc d", p=128)), 16)
                w_after_expert[e + 2] = ws.n

    @block.sync
    def _(s):
        W(s, "s", d_sp)
        d_x.inc(s.dma_start(out=so_sb[:, :, :],
                            in_=sodram.rearrange("(ec p) d -> p ec d", p=128)), 16)

    v_b_tile = {}
    v_pO_release = {}
    peo_bt = {}
    a_bepi_end = {}

    def stb(par, col):
        return st8[:, col * 2 + par: col * 2 + par + 1]

    def stbp(col):
        return st8[:, col * 2: col * 2 + 2]

    v_expT = {}

    def emit_bPE(tix):
        par = tix % 2
        pO = pOa if par == 0 else pOb
        dxp = d_xA if par == 0 else d_xB

        @block.gpsimd
        def _(g, tix=tix, par=par, dxp=dxp):
            W(g, "g", d_sp)
            W(g, "g", d_x)
            if tix >= 2:
                W(g, "g", V, v_b_tile[tix - 2])
            dxp.inc(g.dma_start(out=xp_in[:, par, :], in_=xprime[tix * 128:(tix + 1) * 128, :]), 16)
            dxp.inc(g.dma_start(out=exp_in[:, par, :], in_=expb[tix * 128:(tix + 1) * 128, :]), 16)
        dxp_val = dxp.n

        pD = pTR[:, :] if par == 0 else pAT[:, :].bitcast(BF16)

        @block.tensor
        def _(t, tix=tix, par=par, dxp=dxp, dxp_val=dxp_val, pD=pD):
            W(t, "t", d_x)
            W(t, "t", dxp, dxp_val)
            if tix == 0:
                W(t, "t", V, V.n)
            elif tix == 1:
                W(t, "t", V, v_xnfm[(nblk - 1) * NT + 3])
            else:
                W(t, "t", V, v_expT.get(tix - 2))
            for ec in range(2):
                mm = t.transpose(pD[:, ec * 128:(ec + 1) * 128],
                                 exp_in[:, par, ec * 128:(ec + 1) * 128], ident[:, :])
            peT.inc(mm)

        def f_expT(v, raw, pD=pD):
            W(raw, "v", peT)
            v.tensor_copy(out=expT[:, :, :].rearrange("p a b -> p (a b)"),
                          in_=pD[:, 0:256])
        vector_section(f_expT)
        v_expT[tix] = V.n

        @block.tensor
        def _(t, tix=tix, pO=pO):
            W(t, "t", V)
            if tix < 2:
                W(t, "t", A)
            if tix >= 2:
                W(t, "t", V, v_pO_release[tix - 2])
            for dvh in range(2):
                sl = slice(dvh * 512, (dvh + 1) * 512)
                for ec in range(2):
                    mm = t.matmul(pO[:, sl], expT[:, ec, :], so_sb[:, ec, sl],
                                  start=(ec == 0), stop=(ec == 1))
            peO.inc(mm)
        peo_bt[tix] = peO.n

    def emit_bEpi_pair(tix):
        def f_bstt(v, raw, tix=tix):
            W(raw, "v", peO, peo_bt[tix + 1])
            W(raw, "v", A, a_bepi_end.get(tix - 2))
            for k in range(2):
                v.scalar_tensor_tensor(out=tC[:, k, :], in0=(pOa if k == 0 else pOb)[:, :],
                                       scalar=rcden[:, tix + k:tix + k + 1],
                                       in1=xp_in[:, k, :],
                                       op0=ALU.mult, op1=ALU.add)
                v.reduce_sum(out=stb(k, S1), in_=tC[:, k, :], axis=AX.X)
        vector_section(f_bstt)
        v_pO_release[tix] = V.n
        v_pO_release[tix + 1] = V.n

        def f_bsq(sc, raw, tix=tix):
            if tix == 0:
                pin_act(raw, SET_LNEXP)
            W(raw, "sc", V)
            sc.activation(out=jC[:, :], in_=tC[:, 0, :], func=AF.Square,
                          accum_out=stb(0, S2))
            sc.activation(out=jC[:, :], in_=tC[:, 1, :], func=AF.Square,
                          accum_out=stb(1, S2))
            sc.activation(out=st8[:, 24:28], in_=st8[:, 0:4], func=AF.Identity)
        scalar_section(f_bsq)

        def fv(v, raw):
            W(raw, "v", A)
            v.tensor_scalar_mul(out=stbp(MEAN), in0=stbp(S1), scalar1=INV_C)
            v.tensor_scalar_mul(out=stbp(S2C), in0=stbp(S2), scalar1=INV_C)
            v.tensor_tensor(out=stbp(10), in0=stbp(MEAN), in1=stbp(MEAN), op=ALU.mult)
            v.tensor_tensor(out=stbp(NVAR), in0=stbp(10), in1=stbp(S2C), op=ALU.subtract)
        vector_section(fv)

        def fs(sc, raw):
            W(raw, "sc", V)
            sc.activation(out=stbp(RSTD), in_=stbp(NVAR),
                          func=AF.Ln, scale=-1.0, bias=eps_c[:, 0:1])
            sc.activation(out=stbp(RSTD), in_=stbp(RSTD), func=AF.Exp, scale=-0.5)
        scalar_section(fs)

        def fv2(v, raw):
            W(raw, "v", A)
            for k in range(2):
                v.tensor_scalar(out=stb(k, NMR), in0=stb(k, MEAN),
                                scalar1=stb(k, RSTD), scalar2=-1.0,
                                op0=ALU.mult, op1=ALU.mult)
        vector_section(fv2)

        def f_bout(sc, raw):
            W(raw, "sc", V)
            W(raw, "sc", d_out)
            for k in range(2):
                sc.activation(out=out_sb[:, k, :], in_=tC[:, k, :], func=AF.Identity,
                              scale=stb(k, RSTD), bias=stb(k, NMR))
        scalar_section(f_bout)
        a_bepi_end[tix] = A.n
        a_bepi_end[tix + 1] = A.n
        v_b_tile[tix] = V.n
        v_b_tile[tix + 1] = V.n

        @block.sync
        def _(s, tix=tix):
            W(s, "s", A)
            for k in range(2):
                d_out.inc(s.dma_start(out=out_d[(tix + k) * 128:(tix + k + 1) * 128, :],
                                      in_=out_sb[:, k, :]), 16)

    for tix in range(0, ntile, 2):
        if tix >= 2:
            emit_bEpi_pair(tix - 2)
        emit_bPE(tix)
        emit_bPE(tix + 1)
    emit_bEpi_pair(ntile - 2)


    @block.sync
    def _(s):
        W(s, "s", d_out, 16 * ntile)

    bctx.close()
    ctx.close()
    return nc


def make_in_maps(inp, t_tok=T):
    x = np.asarray(inp["L0_attn_x"], dtype=np.float32)
    f32 = lambda k: np.ascontiguousarray(np.asarray(inp[k], dtype=np.float32))
    chunked = lambda k: np.ascontiguousarray(
        np.asarray(inp[k], dtype=np.float32).reshape(NCH, 128).T)
    rep = lambda k, n: np.ascontiguousarray(
        np.broadcast_to(np.asarray(inp[k], dtype=np.float32).reshape(1, -1), (128, n)))

    mask = np.zeros((128, NT, BLK), dtype=ml_dtypes.bfloat16)
    ii = np.arange(BLK)[None, None, :]
    jj = np.arange(NT)[None, :, None] * 128 + np.arange(128)[:, None, None]
    mask[:] = (ii >= jj).astype(ml_dtypes.bfloat16)

    common = dict(
        w1q=f32("L0_attn_W1q"), w2q=f32("L0_attn_W2q"),
        w1k=f32("L0_attn_W1k"), w2k=f32("L0_attn_W2k"),
        w1v=f32("L0_attn_W1v"), w2v=f32("L0_attn_W2v"),
        b1q=chunked("L0_attn_b1q"), b2q=chunked("L0_attn_b2q"),
        b1k=chunked("L0_attn_b1k"), b2k=chunked("L0_attn_b2k"),
        b1v=chunked("L0_attn_b1v"),
        b2v=np.ascontiguousarray(np.asarray(inp["L0_attn_b2v"], dtype=np.float32).reshape(1, C)),
        lnw=rep("L0_attn_ln_w", C), lnb=rep("L0_attn_ln_b", C),
        g3=rep("L0_ffn_ln_gamma", C), b3=rep("L0_ffn_ln_beta", C),
        ffnscale=np.ascontiguousarray(
            np.broadcast_to(np.asarray(inp["L0_ffn_scale"], dtype=np.float32).reshape(1, 1), (128, 1))),
        mu=np.ascontiguousarray(np.asarray(inp["L0_ffn_mu"], dtype=np.float32).reshape(C, ES)),
        w1e=f32("L0_ffn_w1"),
        b1e=np.ascontiguousarray(np.asarray(inp["L0_ffn_b1"], dtype=np.float32).reshape(1, NE * HE)),
        w2e=f32("L0_ffn_w2"),
        b2e=np.ascontiguousarray(np.asarray(inp["L0_ffn_b2"], dtype=np.float32).reshape(1, NE * C)),
        mask=mask,
        ident=np.eye(128, dtype=ml_dtypes.bfloat16),
        onesr=np.ones((1, 128), dtype=ml_dtypes.bfloat16),
        onesc=np.ones((128, 1), dtype=ml_dtypes.bfloat16),
    )
    in_maps = []
    for c in range(8):
        m = dict(common)
        m["x"] = np.ascontiguousarray(x[c % 4, :t_tok, :])
        in_maps.append(m)
    return in_maps


def kernel(**inp):
    in_maps = make_in_maps(inp)
    nc = build_nc()
    res = run_bass_kernel_spmd(nc, in_maps, core_ids=list(range(8)))
    out = np.stack([np.asarray(res.results[b]["OUT"]) for b in range(4)], axis=0)
    return out.astype(np.float32)
